# version 13
# baseline (speedup 1.0000x reference)
import contextlib

import numpy as np

import concourse.bass as bass
import concourse.mybir as mybir
from concourse.bass_utils import run_bass_kernel_spmd

F32 = mybir.dt.float32
BF16 = mybir.dt.bfloat16
AF = mybir.ActivationFunctionType
ALU = mybir.AluOpType
AX = mybir.AxisListType

D = 1024
L = 2048
H = 16
HD = 64
NCK = 8
SW = 4
TSW = 512
CH = 128
E = 8
IE = 512
SHI = 1024
LN_EPS = 1e-5
RMS_EPS = float(np.finfo(np.float32).eps)
TOPK_SCALE = 2.5
NSLOT = 4
SKIP_DELTA = False
DELTA_SW = None
SKIP_FFN = False


class Sem:
    def __init__(self, name):
        self.name, self.n, self.handle = name, 0, None


class Prog:
    ENGINES = ("tensor", "vector", "scalar", "gpsimd", "sync")

    def __init__(self):
        self.ops = {e: [] for e in self.ENGINES}

    def op(self, engine, fn, wait=(), inc=(), drain=None):
        if drain is None:
            drain = engine in ("vector", "scalar", "gpsimd")
        w = [(s, c) for (s, c) in wait if c > 0]
        for s, amt in inc:
            s.n += amt
        self.ops[engine].append((tuple(w), fn, tuple(inc), drain))

    def emit(self, engine_name, eng):
        first = True
        for waits, fn, incs, drain in self.ops[engine_name]:
            if drain and not first:
                eng.drain()
            for s, c in waits:
                eng.wait_ge(s.handle, c)
            ins = fn(eng)
            for s, amt in incs:
                ins = ins.then_inc(s.handle, amt)
            first = False


class Bank:
    def __init__(self, t):
        self.t = t
        self.free = []

    def ap(self, p0=0, pn=128, c0=0, cn=512):
        return self.t[p0:p0 + pn, c0:c0 + cn]


def host_constants():
    c = {}
    c["i128"] = np.eye(128, dtype=np.float32)
    tau = np.arange(1, CH + 1)
    dif = (tau[:, None] - tau[None, :]).astype(np.float32)
    c["iota_l"] = np.where(dif > 0, dif, 0).astype(np.float32)
    c["iota_u"] = np.ascontiguousarray(c["iota_l"].T)
    c["u01s"] = (c["iota_u"] > 0).astype(np.float32)
    c["u01i"] = (dif.T >= 0).astype(np.float32)
    c["iota_row"] = np.broadcast_to(tau.astype(np.float32)[None, :], (128, 128)).copy()
    c["iota_revcol"] = (CH - tau).astype(np.float32)[:, None].copy()
    c["ones_row"] = np.ones((1, 128), dtype=np.float32)
    c["ones_col"] = np.ones((128, 1), dtype=np.float32)
    hm = np.zeros((NCK, 128, H), dtype=np.float32)
    hs = np.zeros((NCK, H, 128), dtype=np.float32)
    for cc in range(NCK):
        for p in range(128):
            h = 2 * cc + p // 64
            hm[cc, p, h] = 1.0
            hs[cc, h, p] = 1.0
    c["headmask"] = np.ascontiguousarray(hm.transpose(1, 0, 2).reshape(128, NCK * H))
    c["headsel"] = np.ascontiguousarray(hs.transpose(1, 0, 2).reshape(H, NCK * 128))
    c["idxrow8"] = np.broadcast_to(np.arange(E, dtype=np.float32)[None, :], (128, E)).copy()
    c["revidx8"] = (float(E) - c["idxrow8"]).copy()
    es = np.zeros((E, E * 128), dtype=np.float32)
    for e in range(E):
        es[e, e * 128:(e + 1) * 128] = 1.0
    c["esel"] = es
    return c


CONST_SHAPES = {
    "i128": [128, 128], "iota_l": [128, 128], "iota_u": [128, 128],
    "u01s": [128, 128], "u01i": [128, 128], "iota_row": [128, 128],
    "iota_revcol": [128, 1], "ones_row": [1, 128], "ones_col": [128, 1],
    "headmask": [128, NCK * H], "headsel": [H, NCK * 128],
    "idxrow8": [128, E], "revidx8": [128, E], "esel": [E, E * 128],
}


def build_program(debug=None):
    nc = bass.Bass("TRN2", target_bir_lowering=False)
    dt = nc.dram_tensor

    xt = dt("xt", [D, L], F32, kind="ExternalInput").ap()
    wT = dt("wT", [D, D], F32, kind="ExternalInput").ap()
    ws_gT = dt("ws_gT", [D, SHI], F32, kind="ExternalInput").ap()
    ws_uT = dt("ws_uT", [D, SHI], F32, kind="ExternalInput").ap()
    ws_dT = dt("ws_dT", [SHI, D], F32, kind="ExternalInput").ap()
    w_gT = dt("w_gT", [E * D, IE], F32, kind="ExternalInput").ap()
    w_uT = dt("w_uT", [E * D, IE], F32, kind="ExternalInput").ap()
    w_dT = dt("w_dT", [E * IE, D], F32, kind="ExternalInput").ap()
    gwT = dt("gwT", [D, E], F32, kind="ExternalInput").ap()
    onwT = dt("onwT", [128, NCK], F32, kind="ExternalInput").ap()
    onbT = dt("onbT", [128, NCK], F32, kind="ExternalInput").ap()
    knw_d = dt("knw", [128, 1], F32, kind="ExternalInput").ap()
    eta_d = dt("eta_row", [1, H], F32, kind="ExternalInput").ap()
    laml_d = dt("laml_row", [1, H], F32, kind="ExternalInput").ap()
    cst = {n: dt(n, s, F32, kind="ExternalInput").ap() for n, s in CONST_SHAPES.items()}
    ws_gTh = dt("ws_gTh", [D, SHI], BF16).ap()
    ws_gTl = dt("ws_gTl", [D, SHI], BF16).ap()
    ws_uTh = dt("ws_uTh", [D, SHI], BF16).ap()
    ws_uTl = dt("ws_uTl", [D, SHI], BF16).ap()
    ws_dTh = dt("ws_dTh", [SHI, D], BF16).ap()
    ws_dTl = dt("ws_dTl", [SHI, D], BF16).ap()
    w_gTh = dt("w_gTh", [E * D, IE], BF16).ap()
    w_gTl = dt("w_gTl", [E * D, IE], BF16).ap()
    w_uTh = dt("w_uTh", [E * D, IE], BF16).ap()
    w_uTl = dt("w_uTl", [E * D, IE], BF16).ap()
    w_dTh = dt("w_dTh", [E * IE, D], BF16).ap()
    w_dTl = dt("w_dTl", [E * IE, D], BF16).ap()
    yt = dt("yt", [D, L], F32, kind="ExternalOutput").ap()
    dbg_ap = None
    if debug in ("static", "k", "attnpre", "x1"):
        dbg_ap = dt("dbg", [D, L], F32, kind="ExternalOutput").ap()
    elif debug == "comb":
        dbg_ap = dt("dbg", [E, L], F32, kind="ExternalOutput").ap()

    prog = Prog()
    s_ldc = Sem("ldc")
    s_ldx = Sem("ldx")
    s_sl = [Sem(f"sl{i}") for i in range(NSLOT)]
    s_t = Sem("t")
    s_v = Sem("v")
    s_s = Sem("s")
    s_out = Sem("out")
    s_dbg = Sem("dbg")
    s_g = Sem("g")
    s_gc = Sem("gc")
    s_wsp = Sem("wsp")
    s_wspl = [Sem(f"wspl{i}") for i in range(NSLOT)]
    sems = [s_ldc, s_ldx, s_t, s_v, s_s, s_out, s_dbg, s_g, s_gc, s_wsp] + s_sl + s_wspl

    stack = contextlib.ExitStack()

    def sb(name, shape, dtype=F32):
        return stack.enter_context(nc.sbuf_tensor(name, shape, dtype))

    c_sb = {n: sb("c_" + n, CONST_SHAPES[n]) for n in CONST_SHAPES}
    c_onw = sb("c_onw", [128, NCK])
    c_onb = sb("c_onb", [128, NCK])
    c_knw = sb("c_knw", [128, 1])
    c_eps = sb("c_eps", [128, 2])
    hrow = sb("hrow", [1, H * 6])
    ph = sb("ph", [128, H * 3])
    strow = sb("strow", [1, 3 * TSW])
    st16 = sb("st16", [16, 2 * TSW])
    sqt = sb("sqt", [128, 2 * TSW])
    gwsb = sb("gwsb", [128, NCK * E])
    gwh = sb("gwh", [128, NCK * E], BF16)
    gwl = sb("gwl", [128, NCK * E], BF16)
    x1h = sb("x1h", [128, NCK * L], BF16)
    x1l = sb("x1l", [128, NCK * L], BF16)
    ring = sb("ring", [128, NSLOT * 1024])

    AR = 24592
    arena = sb("arena", [128, AR])

    class Off:
        def __init__(self):
            self.o = 0

        def take(self, n):
            r = self.o
            self.o += n
            assert self.o <= AR, f"arena overflow {self.o}"
            return r

    att = Off()
    o_xsw = att.take(NCK * TSW)
    o_bsw = att.take(NCK * TSW)
    o_ksw = att.take(NCK * TSW)
    o_maskY = att.take(H * CH)
    o_masknA = att.take(H * CH)
    o_lb = att.take(H * CH)
    o_nblr = att.take(H)
    o_sstate = att.take(H * HD)
    o_slot = []
    for s_i in range(1):
        d = {"ypow": att.take(5 * 2 * CH), "yt": att.take(2 * 2 * CH),
             "nale": att.take(2 * CH), "dkf": att.take(2 * CH),
             "nbk": att.take(CH), "ktm": att.take(CH)}
        o_slot.append(d)
    o_nblrp = att.take(H * HD)
    o_xchp = att.take(2 * CH)
    o_rtp = att.take(CH)
    o_otmp = att.take(CH)
    o_x1s = att.take(2 * TSW)

    ffn = Off()
    o_sgh = ffn.take(NCK * TSW)
    o_acc = ffn.take(NCK * TSW)
    o_comb = ffn.take(L)
    o_t8 = ffn.take(16 * E)
    o_r8 = ffn.take(64)
    o_outst = ffn.take(TSW)
    o_hhh = ffn.take(NCK * TSW // 2)
    o_hhl = ffn.take(NCK * TSW // 2)

    def A(off, p0, pn, c0, cn):
        return arena[p0:p0 + pn, off + c0:off + c0 + cn]

    def AB(off, f32cols):
        return arena[:, off:off + f32cols].bitcast(BF16)



    hhh = AB(o_hhh, NCK * TSW // 2)
    hhl = AB(o_hhl, NCK * TSW // 2)

    pbank = [Bank(stack.enter_context(nc.psum_tensor(f"pb{i}", [128, 512], F32)))
             for i in range(8)]

    def mm(bank, lhsT, rhs, start, stop, wait=(), p0=0, pn=128, c0=0, cn=None):
        cn = cn if cn is not None else rhs.shape[-1]
        w = list(wait)
        if start:
            w += bank.free
            bank.free = []
        prog.op("tensor",
                lambda e, b=bank, l2=lhsT, r2=rhs, st=start, sp=stop, q=(p0, pn, c0, cn):
                e.matmul(b.ap(*q), l2, r2, start=st, stop=sp),
                wait=w, inc=[(s_t, 1)])
        return s_t.n

    def tr(bank, in_, ident, wait=(), p0=0, pn=None, c0=0, cn=None, start=True,
           stop=True):
        pn = pn if pn is not None else in_.shape[-1]
        cn = cn if cn is not None else in_.shape[0]
        w = list(wait)
        if start:
            w += bank.free
            bank.free = []
        prog.op("tensor",
                lambda e, b=bank, i2=in_, id2=ident, q=(p0, pn, c0, cn), st=start, sp=stop:
                e.matmul(b.ap(*q), i2, id2, start=st, stop=sp, is_transpose=True),
                wait=w, inc=[(s_t, 1)])
        return s_t.n

    def ev(engine, fn, bank=None, wait=()):
        sem = s_v if engine == "vector" else s_s
        prog.op(engine, fn, wait=wait, inc=[(sem, 1)])
        if bank is not None:
            bank.free.append((sem, sem.n))
        return sem.n

    def dma(engine, out_ap, in_ap, wait=(), sem=None):
        sem = sem or s_ldc
        prog.op(engine, lambda e, o=out_ap, i=in_ap: e.dma_start(out=o, in_=i),
                wait=wait, inc=[(sem, 16)], drain=False)
        return sem.n

    sq_rd = [0, 0]

    for n in CONST_SHAPES:
        dma("sync", c_sb[n][:, :], cst[n])
    dma("sync", c_onw[:, :], onwT)
    dma("sync", c_onb[:, :], onbT)
    dma("sync", c_knw[:, :], knw_d)
    dma("sync", hrow[:1, 0:H], eta_d)
    dma("sync", hrow[:1, H:2 * H], laml_d)
    dma("sync", gwsb[:, :].rearrange("p (c w) -> p c w", c=NCK),
        gwT.rearrange("(c p) w -> p c w", p=128))
    ld_const = s_ldc.n

    def hr(i):
        return hrow[:1, i * H:(i + 1) * H]

    I128 = c_sb["i128"][:, :]
    I64 = c_sb["i128"][:64, :64]

    def ident64(pr):
        return c_sb["i128"][pr:pr + 64, pr:pr + 64]
    ONESR = c_sb["ones_row"][:, :]
    ONESC = c_sb["ones_col"][:, :]

    prog.op("vector", lambda e: e.memset(c_eps[:, 0:1], RMS_EPS))
    prog.op("vector", lambda e: e.memset(c_eps[:, 1:2], LN_EPS))
    prog.op("scalar", lambda e: e.activation(hr(0), hr(0), AF.Tanh),
            wait=[(s_ldc, ld_const)])
    prog.op("scalar", lambda e: e.activation(hr(1), hr(1), AF.Sigmoid))
    prog.op("scalar", lambda e: e.activation(hr(3), hr(1), AF.Ln), inc=[(s_s, 1)])
    sw0 = s_s.n
    prog.op("scalar", lambda e: e.activation(hr(4), hr(3), AF.Exp, scale=float(CH)),
            inc=[(s_s, 1)])
    prog.op("vector", lambda e: e.tensor_tensor(hr(2), hr(0), hr(1), ALU.mult),
            wait=[(s_s, sw0)])
    prog.op("vector", lambda e: e.tensor_sub(hr(2), hr(2), hr(0)))
    prog.op("vector", lambda e: e.tensor_scalar_mul(hr(2), hr(2), 1.0 / HD))
    prog.op("vector", lambda e: e.tensor_scalar_mul(hr(5), hr(2), -1.0), inc=[(s_v, 1)])
    v_hr, s_hr = s_v.n, s_s.n
    b0 = pbank[0]
    mm(b0, ONESR, hr(3), True, False, wait=[(s_v, v_hr), (s_s, s_hr)], c0=0, cn=H)
    mm(b0, ONESR, hr(5), False, False, c0=H, cn=H)
    t_ph = mm(b0, ONESR, hr(4), False, True, c0=2 * H, cn=H)
    ev("vector", lambda e: e.tensor_copy(ph[:, :], b0.ap(0, 128, 0, 3 * H)),
       bank=b0, wait=[(s_t, t_ph)])
    v_ph = s_v.n

    def PH(i, h):
        return ph[:, i * H + h:i * H + h + 1]

    for h in range(H):
        mY = A(o_maskY, 0, 128, h * CH, CH)
        mA_ = A(o_masknA, 0, 128, h * CH, CH)
        lbh = A(o_lb, 0, 64, h * CH, CH)
        nbh = A(o_nblr, 0, 128, h, 1)
        prog.op("scalar", lambda e, o=mY, s1=PH(0, h): e.activation(
            o, c_sb["iota_u"][:, :], AF.Exp, scale=s1),
            wait=[(s_v, v_ph)], inc=[(s_s, 1)])
        sw1 = s_s.n
        prog.op("vector", lambda e, o=mA_, i=mY, s1=PH(1, h): e.scalar_tensor_tensor(
            o, i, s1, c_sb["u01i"][:, :], ALU.mult, ALU.mult),
            wait=[(s_s, sw1)])
        prog.op("vector", lambda e, o=mY, s1=PH(1, h): e.scalar_tensor_tensor(
            o, o, s1, c_sb["u01s"][:, :], ALU.mult, ALU.mult), inc=[(s_v, 1)])
        prog.op("scalar", lambda e, o=lbh, s1=PH(0, h): e.activation(
            o, c_sb["iota_row"][:64, :], AF.Exp, scale=s1[:64, :]),
            inc=[(s_s, 1)])
        prog.op("scalar", lambda e, o=nbh, s1=PH(0, h): e.activation(
            o, c_sb["iota_revcol"][:, :], AF.Exp, scale=s1), inc=[(s_s, 1)])
        sw2 = s_s.n
        prog.op("vector", lambda e, o=nbh, s1=PH(1, h): e.tensor_scalar(
            o, o, s1, None, ALU.mult), wait=[(s_s, sw2)])
        prog.op("vector", lambda e, o=A(o_sstate, 0, 64, h * HD, HD): e.memset(o, 0.0),
                inc=[(s_v, 1)])
        nbp = A(o_nblrp, 0, 128, (h // 2) * CH + (h % 2) * HD, HD)
        prog.op("vector", lambda e, o=nbp: e.memset(o, 1.0))
        prog.op("vector", lambda e, o=nbp, s1=nbh: e.tensor_scalar(
            o, o, s1, None, ALU.mult), inc=[(s_v, 1)])
    setup_v, setup_s = s_v.n, s_s.n

    ring_free = [0] * NSLOT
    rsl = [0]

    def slab_cols(sl, i):
        return ring[:, sl * 1024 + i * 128: sl * 1024 + (i + 1) * 128]

    ring_f32_rd = [0] * NSLOT
    gsub_flip = [0]
    split_jobs = []

    def make_split_jobs():
        mats = [(ws_gT, ws_gTh, ws_gTl, D, SHI), (ws_uT, ws_uTh, ws_uTl, D, SHI),
                (ws_dT, ws_dTh, ws_dTl, SHI, D), (w_gT, w_gTh, w_gTl, E * D, IE),
                (w_uT, w_uTh, w_uTl, E * D, IE), (w_dT, w_dTh, w_dTl, E * IE, D)]
        for mf, mh, ml, rows, cols in mats:
            for r0 in range(0, rows, 128):
                split_jobs.append((mf, mh, ml, r0, cols))

    def emit_split_job():
        if not split_jobs:
            return
        mf, mh, ml, r0, cols = split_jobs.pop(0)
        sl = rsl[0] % NSLOT
        rsl[0] += 1
        slf = ring[:, sl * 1024: sl * 1024 + cols]
        dma("sync", slf, mf[r0:r0 + 128, :], wait=[(s_t, ring_free[sl])],
            sem=s_sl[sl])
        ld = s_sl[sl].n
        sl2 = rsl[0] % NSLOT
        rsl[0] += 1
        stage = ring[:, sl2 * 1024: sl2 * 1024 + 1024].bitcast(BF16)
        sh2 = stage[:, 0:cols]
        sl3 = stage[:, 1024:1024 + cols]
        prog.op("scalar", lambda e, o=sh2, i2=slf: e.copy(o, i2),
                wait=[(s_sl[sl], ld), (s_t, ring_free[sl2]),
                      (s_wspl[sl2], ring_wsp_rd[sl2])], inc=[(s_s, 1)])
        ssp = s_s.n
        prog.op("vector", lambda e, o=sl3, a=slf, b=sh2: e.tensor_sub(o, a, b),
                wait=[(s_s, ssp)], inc=[(s_v, 1)])
        vv = s_v.n
        ring_free_f = s_v.n
        dma("sync", mh[r0:r0 + 128, :], sh2, wait=[(s_v, vv)], sem=s_wspl[sl2])
        dma("sync", ml[r0:r0 + 128, :], sl3, sem=s_wspl[sl2])
        ring_wsp_rd[sl2] = s_wspl[sl2].n
        ring_f32_rd[sl] = 0
        ring_vrd[sl] = ring_free_f

    ring_wsp_rd = [0] * NSLOT
    ring_vrd = [0] * NSLOT

    def proj3(mat_hl, n_out, kdim, rhs_fn, banks, evac, wait_first=()):
        math_ap, matl_ap = mat_hl
        for oc in range(n_out):
            sl = rsl[0] % NSLOT
            rsl[0] += 1
            view = ring[:, sl * 1024: sl * 1024 + 1024].bitcast(BF16)
            srch = math_ap[:, oc * 128:(oc + 1) * 128].rearrange(
                "(c p) j -> p c j", p=128)
            srcl = matl_ap[:, oc * 128:(oc + 1) * 128].rearrange(
                "(c p) j -> p c j", p=128)
            wfree = [(s_t, ring_free[sl]), (s_v, ring_vrd[sl]),
                     (s_wspl[sl], ring_wsp_rd[sl])]
            dma("sync", view[:, 0:kdim * 128].rearrange("p (c j) -> p c j", c=kdim),
                srch, wait=wfree, sem=s_sl[sl])
            dma("sync", view[:, 1024:1024 + kdim * 128].rearrange(
                "p (c j) -> p c j", c=kdim), srcl, sem=s_sl[sl])
            ld = s_sl[sl].n
            bank = banks[oc % len(banks)]
            t_l = 0
            for i in range(kdim):
                w = [(s_sl[sl], ld)]
                if i == 0:
                    w += list(wait_first)
                xh, xl = rhs_fn(i)
                wh = view[:, i * 128:(i + 1) * 128]
                wl = view[:, 1024 + i * 128:1024 + (i + 1) * 128]
                mm(bank, wh, xh, i == 0, False, wait=w)
                mm(bank, wh, xl, False, False)
                t_l = mm(bank, wl, xh, False, i == kdim - 1)
            ring_free[sl] = t_l
            evac(oc, bank, t_l)

    def proj(mat_ap, n_out, kdim, rhs_fn, banks, evac, wait_first=()):
        for oc in range(n_out):
            sl = rsl[0] % NSLOT
            rsl[0] += 1
            src = mat_ap[:, oc * 128:(oc + 1) * 128].rearrange(
                "(c p) j -> p c j", p=128)
            dst = ring[:, sl * 1024: sl * 1024 + kdim * 128].rearrange(
                "p (c j) -> p c j", c=kdim)
            dma("sync", dst, src, wait=[(s_t, ring_free[sl]), (s_v, ring_vrd[sl]),
                                        (s_wspl[sl], ring_wsp_rd[sl])], sem=s_sl[sl])
            ld = s_sl[sl].n
            bank = banks[oc % len(banks)]
            t_l = 0
            for i in range(kdim):
                w = [(s_sl[sl], ld)]
                if i == 0:
                    w += list(wait_first)
                t_l = mm(bank, slab_cols(sl, i), rhs_fn(i), i == 0, i == kdim - 1,
                         wait=w)
            ring_free[sl] = t_l
            evac(oc, bank, t_l)

    MUSQ = strow[:1, 2 * TSW:3 * TSW]

    def ln_stats(src_fn, mu_ap, r_ap, first_wait, b_sum, b_sq):
        for c in range(NCK):
            w = list(first_wait) if c == 0 else []
            mm(b_sum, ONESC, src_fn(c), c == 0, c == NCK - 1, wait=w, pn=1)
        for c in range(NCK):
            half = c % 2
            sq = sqt[:, half * TSW: half * TSW + TSW]
            w = [(s_t, sq_rd[half])] + (list(first_wait) if c == 0 else [])
            prog.op("scalar", lambda e, s2=sq, c2=c, fn=src_fn: e.activation(
                s2, fn(c2), AF.Square), wait=w, inc=[(s_s, 1)])
            ssq = s_s.n
            sq_rd[half] = mm(b_sq, ONESC, sq, c == 0, c == NCK - 1,
                             wait=[(s_s, ssq)], pn=1)
        t_st = s_t.n
        prog.op("scalar", lambda e, m2=mu_ap: e.activation(
            m2, b_sum.ap(0, 1, 0, TSW), AF.Copy, scale=1.0 / D),
            wait=[(s_t, t_st)], inc=[(s_s, 1)])
        b_sum.free = [(s_s, s_s.n)]
        prog.op("scalar", lambda e, r2=r_ap: e.activation(
            r2, b_sq.ap(0, 1, 0, TSW), AF.Copy, scale=1.0 / D), inc=[(s_s, 1)])
        b_sq.free = [(s_s, s_s.n)]
        s_m = s_s.n
        prog.op("vector", lambda e, m2=mu_ap: e.tensor_tensor(MUSQ, m2, m2, ALU.mult),
                wait=[(s_s, s_m)])
        prog.op("vector", lambda e, r2=r_ap: e.tensor_sub(r2, r2, MUSQ), inc=[(s_v, 1)])
        v_va = s_v.n
        prog.op("scalar", lambda e, r2=r_ap: e.activation(r2, r2, AF.Sqrt,
                                                         bias=c_eps[:1, 1:2]),
                wait=[(s_v, v_va)], inc=[(s_s, 1)])
        s_q = s_s.n
        prog.op("vector", lambda e, r2=r_ap: e.reciprocal(r2, r2),
                wait=[(s_s, s_q)], inc=[(s_v, 1)])
        return s_v.n

    def xsw(c, t0=0, n=TSW, p0=0, pn=128):
        return A(o_xsw, p0, pn, c * TSW + t0, n)

    def bsw(c, t0=0, n=TSW, p0=0, pn=128):
        return A(o_bsw, p0, pn, c * TSW + t0, n)

    def ksw(c, t0=0, n=TSW, p0=0, pn=128):
        return A(o_ksw, p0, pn, c * TSW + t0, n)

    make_split_jobs()
    x_cons = {"t": 0, "s": 0, "v": 0}
    b_cons = {"t": 0, "s": 0}
    head_dep = {h: [] for h in range(H)}
    slot_free_t = [0, 0]
    x1_done = [0] * SW

    def delta_pair_emit(hp, ci, k_dep, static_dep):
        h0, h1 = hp, hp + 1
        cc = hp // 2
        d = o_slot[0]
        PA, PT = pbank[2], pbank[3]
        PQ, PS = pbank[6], pbank[7]

        def yp(k2, half=None):
            if half is None:
                return A(d["ypow"], 0, 128, (k2 % 5) * 2 * CH, 2 * CH)
            return A(d["ypow"], 0, 128, (k2 % 5) * 2 * CH + half * CH, CH)

        def ytp(k2, half=None):
            if half is None:
                return A(d["yt"], 0, 128, (k2 % 2) * 2 * CH, 2 * CH)
            return A(d["yt"], 0, 128, (k2 % 2) * 2 * CH + half * CH, CH)

        nalp = A(d["nale"], 0, 128, 0, 2 * CH)
        dkfp = A(d["dkf"], 0, 64, 0, 2 * CH)
        nbkp = A(d["nbk"], 0, 128, 0, CH)
        ktmp = A(d["ktm"], 0, 128, 0, CH)
        mYp = A(o_maskY, 0, 128, h0 * CH, 2 * CH)
        mAp = A(o_masknA, 0, 128, h0 * CH, 2 * CH)
        lbp = A(o_lb, 0, 64, h0 * CH, 2 * CH)
        nbp = A(o_nblrp, 0, 128, cc * CH, CH)
        kf0_ = ksw(cc, ci * CH, CH, 0, 64)
        kf1_ = ksw(cc, ci * CH, CH, 64, 64)
        bpair = bsw(cc, ci * CH, CH)
        rtp = A(o_rtp, 0, 128, 0, CH)
        otp = A(o_otmp, 0, 128, 0, CH)

        def xch(i):
            return A(o_xchp, 0, 128, (i % 2) * CH, CH)

        sst0 = A(o_sstate, 0, 64, h0 * HD, HD)
        sst1 = A(o_sstate, 0, 64, h1 * HD, HD)

        gate = [(s_t, slot_free_t[0])]
        t_a0 = mm(PA, kf0_, kf0_, True, True, wait=k_dep + gate, c0=0, cn=CH)
        t_k0 = tr(PT, kf0_, ident64(0), wait=gate, c0=0, start=True, stop=True)
        t_a = mm(PA, kf1_, kf1_, True, True, wait=[(s_t, t_a0)], c0=CH, cn=CH)
        t_ktm = tr(PT, kf1_, ident64(64), c0=64, start=True, stop=True,
                   wait=[(s_t, t_k0)])
        ev("vector", lambda e: e.tensor_tensor(yp(0), PA.ap(0, 128, 0, 2 * CH),
                                               mYp, ALU.mult),
           wait=[(s_t, t_a)])
        ev("vector", lambda e: e.tensor_tensor(nalp, PA.ap(0, 128, 0, 2 * CH),
                                               mAp, ALU.mult), bank=PA)
        v_y = s_v.n
        PA.free.append((s_v, v_y))
        ev("vector", lambda e: e.tensor_tensor(nbkp, PT.ap(0, 128, 0, CH),
                                               nbp, ALU.mult),
           bank=PT, wait=[(s_t, t_ktm)])
        v_nbk = s_v.n
        ev("vector", lambda e: e.tensor_copy(ktmp, PT.ap(0, 128, 0, CH)),
           bank=PT, wait=[(s_t, t_ktm)])
        v_ktm = s_v.n
        tr(PT, ktmp[:, 0:HD], I128, wait=[(s_v, v_ktm)], c0=0, pn=64,
           start=True, stop=False)
        t_kf0 = tr(PT, ktmp[:, HD:2 * HD], I128, c0=CH, pn=64,
                   start=False, stop=True)
        ev("vector", lambda e: e.tensor_tensor(dkfp, PT.ap(0, 64, 0, 2 * CH),
                                               lbp, ALU.mult),
           bank=PT, wait=[(s_t, t_kf0)])
        v_dkf = s_v.n
        tr(PT, yp(0, 0), I128, wait=[(s_v, v_y)], c0=0, start=True, stop=False)
        t_yt = tr(PT, yp(0, 1), I128, c0=CH, start=False, stop=True)
        ev("vector", lambda e: e.tensor_copy(ytp(0), PT.ap(0, 128, 0, 2 * CH)),
           bank=PT, wait=[(s_t, t_yt)])
        sq_d = [(s_v, s_v.n)]
        ytd = [(s_v, s_v.n)]

        tr(PQ, bpair, I128, wait=head_dep[h0] + head_dep[h1] + list(static_dep),
           c0=0, start=True, stop=False)
        mm(PQ, dkfp[:, 0:CH], sst0, False, False, wait=[(s_v, v_dkf)], c0=0, cn=HD)
        t_rt = mm(PQ, dkfp[:, CH:2 * CH], sst1, False, True, c0=64, cn=HD)
        ev("vector", lambda e: e.tensor_copy(rtp, PQ.ap(0, 128, 0, CH)),
           wait=[(s_t, t_rt)])
        apply_dep = [(s_v, s_v.n)]
        cur = [rtp]

        def emit_apply(l, t_mid=None):
            X = cur[0]
            w = list(apply_dep) + sq_d
            t_p0 = mm(PQ, yp(l, 0), X[:, 0:HD], True, True, wait=w,
                      c0=128, cn=HD)
            if t_mid is not None:
                t_mid(t_p0)
            t_ap = mm(PQ, yp(l, 1), X[:, HD:2 * HD], True, True,
                      wait=[(s_t, t_p0)], c0=192, cn=HD)
            dst = xch(l)
            ev("vector", lambda e, ds=dst, x2=X: e.tensor_tensor(
                ds, PQ.ap(0, 128, 128, CH), x2, ALU.add),
               wait=[(s_t, t_ap)])
            apply_dep[:] = [(s_v, s_v.n)]
            cur[0] = dst

        for lvl in range(1, 7):
            t_q0 = mm(PA, ytp(lvl - 1, 0), yp(lvl - 1, 0), True, True,
                      wait=sq_d + ytd, c0=0, cn=CH)
            t_t0 = None
            if lvl < 6:
                t_t0 = mm(PT, yp(lvl - 1, 0), ytp(lvl - 1, 0), True, True,
                          wait=sq_d + ytd, c0=0, cn=CH)
            pend = []

            def mid(t_p0, l2=lvl, q0=t_q0, t0=t_t0):
                t_sq = mm(PA, ytp(l2 - 1, 1), yp(l2 - 1, 1), True, True,
                          wait=[(s_t, q0)], c0=CH, cn=CH)
                pend.append(t_sq)
                if t0 is not None:
                    t_sqt = mm(PT, yp(l2 - 1, 1), ytp(l2 - 1, 1), True, True,
                               wait=[(s_t, t0)], c0=CH, cn=CH)
                    pend.append(t_sqt)

            emit_apply(lvl - 1, t_mid=mid)
            t_sq = pend[0]
            ev("vector", lambda e, l2=lvl: e.tensor_copy(
                yp(l2), PA.ap(0, 128, 0, 2 * CH)), bank=PA, wait=[(s_t, t_sq)])
            sq_d[:] = [(s_v, s_v.n)]
            if lvl < 6:
                t_sqt = pend[1]
                ev("vector", lambda e, l2=lvl: e.tensor_copy(
                    ytp(l2), PT.ap(0, 128, 0, 2 * CH)), bank=PT, wait=[(s_t, t_sqt)])
                ytd[:] = [(s_v, s_v.n)]
        emit_apply(6)
        wt0 = cur[0][:, 0:HD]
        wt1 = cur[0][:, HD:2 * HD]
        t_o0 = mm(PQ, nalp[:, 0:CH], wt0, True, True, wait=list(apply_dep),
                  c0=256, cn=HD)
        t_s0 = mm(PS, nbkp[:, 0:HD], wt0, True, True, wait=[(s_v, v_nbk)],
                  c0=0, cn=HD, pn=64)
        t_o = mm(PQ, nalp[:, CH:2 * CH], wt1, True, True, wait=[(s_t, t_o0)],
                 c0=320, cn=HD)
        t_s2 = mm(PS, nbkp[:, HD:2 * HD], wt1, True, True, wait=[(s_t, t_s0)],
                  c0=64, cn=HD, pn=64)
        ev("vector", lambda e: e.tensor_tensor(otp, PQ.ap(0, 128, 256, CH), rtp,
                                               ALU.add), wait=[(s_t, t_o)])
        v_ot = s_v.n
        t_of = tr(PQ, otp, I128, wait=[(s_v, v_ot)], c0=384, start=True, stop=True)
        ev("vector", lambda e: e.tensor_copy(bpair, PQ.ap(0, 128, 384, CH)),
           wait=[(s_t, t_of)] + list(dbg_gate[0]))
        v_b = s_v.n
        ev("vector", lambda e: e.scalar_tensor_tensor(
            sst0, sst0, PH(2, h0)[:64, :], PS.ap(0, 64, 0, HD),
            ALU.mult, ALU.add), wait=[(s_t, t_s2)])
        ev("vector", lambda e: e.scalar_tensor_tensor(
            sst1, sst1, PH(2, h1)[:64, :], PS.ap(0, 64, 64, HD),
            ALU.mult, ALU.add))
        head_dep[h0] = [(s_v, s_v.n)]
        head_dep[h1] = [(s_v, s_v.n)]
        slot_free_t[0] = s_t.n
        PQ.free = [(s_v, v_b)]
        PS.free = [(s_v, s_v.n)]

    dbg_gate = [()]

    def dump_sw(off, sw, wait):
        t0 = sw * TSW
        for c in range(NCK):
            prog.op("gpsimd", lambda e, c2=c: e.dma_start(
                out=dbg_ap[c2 * 128:(c2 + 1) * 128, t0:t0 + TSW],
                in_=A(off, 0, 128, c2 * TSW, TSW)),
                wait=(wait if c == 0 else []), inc=[(s_dbg, 16)], drain=False)
        dbg_gate[0] = ((s_dbg, s_dbg.n),)

    MU = strow[:1, 0:TSW]
    RR = strow[:1, TSW:2 * TSW]

    for sw in range(SW):
        t0g = sw * TSW
        xw = [(s_t, x_cons["t"]), (s_s, x_cons["s"]), (s_v, x_cons["v"])]
        for c in range(NCK):
            dma("sync", xsw(c), xt[c * 128:(c + 1) * 128, t0g:t0g + TSW],
                wait=(xw if c == 0 else ()), sem=s_ldx)
        x_ld = s_ldx.n

        wf = [(s_ldx, x_ld), (s_t, b_cons["t"]), (s_s, b_cons["s"])]
        if sw == 0:
            wf += [(s_v, setup_v), (s_s, setup_s)]

        def ev_static(oc, bank, tl):
            ev("vector", lambda e, o2=oc, b2=bank: e.tensor_copy(bsw(o2), b2.ap()),
               bank=bank, wait=[(s_t, tl)])
        proj(wT, NCK, NCK, lambda i: xsw(i), [pbank[0], pbank[1]], ev_static,
             wait_first=wf)
        static_v = s_v.n
        x_cons["t"] = s_t.n
        if debug == "static":
            dump_sw(o_bsw, sw, [(s_v, static_v)])

        b6, b7 = pbank[6], pbank[7]
        for c in range(NCK):
            half = c % 2
            sq = sqt[:, half * TSW: half * TSW + TSW]
            w = [(s_ldx, x_ld), (s_t, sq_rd[half])]
            if c == 0 and sw == 0:
                w += [(s_v, setup_v), (s_s, setup_s)]
            prog.op("scalar", lambda e, s2=sq, c2=c: e.activation(
                s2, xsw(c2), AF.Square), wait=w, inc=[(s_s, 1)])
            ssq = s_s.n
            sq_rd[half] = mm(b6, c_sb["headmask"][:, c * H:(c + 1) * H], sq,
                             c == 0, c == NCK - 1, wait=[(s_s, ssq)], pn=16)
        t_ss = s_t.n
        prog.op("scalar", lambda e: e.activation(st16[:, 0:TSW], b6.ap(0, 16, 0, TSW),
                                                 AF.Sqrt, bias=c_eps[:16, 0:1], scale=1.0 / HD),
                wait=[(s_t, t_ss)], inc=[(s_s, 1)])
        b6.free = [(s_s, s_s.n)]
        s_r = s_s.n
        prog.op("vector", lambda e: e.reciprocal(st16[:, TSW:2 * TSW], st16[:, 0:TSW]),
                wait=[(s_s, s_r)], inc=[(s_v, 1)])
        v_r = s_v.n
        for c in range(NCK):
            t_bc = mm(b7, c_sb["headsel"][:, c * 128:(c + 1) * 128],
                      st16[:, TSW:2 * TSW], True, True, wait=[(s_v, v_r)])
            ev("vector", lambda e, c2=c: e.scalar_tensor_tensor(
                ksw(c2), xsw(c2), c_knw[:, :], b7.ap(), ALU.mult, ALU.mult),
               bank=b7, wait=[(s_t, t_bc)] + list(dbg_gate[0]))
        k_v = s_v.n
        x_cons["s"] = s_s.n
        x_cons["v"] = max(x_cons["v"], s_v.n)
        if debug == "k":
            dump_sw(o_ksw, sw, [(s_v, k_v)])

        for ciw in range(TSW // CH):
            for hp in range(0, H, 2):
                delta_pair_emit(hp, ciw, [(s_v, k_v)], [(s_v, static_v)])
                emit_split_job()
                emit_split_job()
        ddep = [(s_v, s_v.n), (s_s, s_s.n), (s_t, s_t.n)]
        if debug == "attnpre":
            dump_sw(o_bsw, sw, ddep)

        v_st = ln_stats(lambda c: bsw(c), MU, RR, ddep, pbank[0], pbank[1])
        mm(pbank[0], ONESR, MU, True, True, wait=[(s_v, v_st)])
        t_bc = mm(pbank[1], ONESR, RR, True, True)
        for c in range(NCK):
            w = ([(s_t, t_bc)] + list(dbg_gate[0])) if c == 0 else []
            prog.op("vector", lambda e, c2=c: e.tensor_sub(bsw(c2), bsw(c2),
                                                           pbank[0].ap()), wait=w)
            prog.op("vector", lambda e, c2=c: e.scalar_tensor_tensor(
                bsw(c2), bsw(c2), c_onw[:, c2:c2 + 1], pbank[1].ap(),
                ALU.mult, ALU.mult))
            prog.op("vector", lambda e, c2=c: e.scalar_tensor_tensor(
                bsw(c2), bsw(c2), c_onb[:, c2:c2 + 1], xsw(c2), ALU.add, ALU.add),
                inc=[(s_v, 1)])
        v_z = s_v.n
        x_cons["v"] = s_v.n
        pbank[0].free = [(s_v, v_z)]
        pbank[1].free = [(s_v, v_z)]
        v_st2 = ln_stats(lambda c: bsw(c), MU, RR, [(s_v, v_z)], pbank[0], pbank[1])
        mm(pbank[0], ONESR, MU, True, True, wait=[(s_v, v_st2)])
        t_bc2 = mm(pbank[1], ONESR, RR, True, True)
        for c in range(NCK):
            w = [(s_t, t_bc2)] if c == 0 else []
            xhc = x1h[:, c * L + t0g: c * L + t0g + TSW]
            xlc = x1l[:, c * L + t0g: c * L + t0g + TSW]
            x1sc = A(o_x1s, 0, 128, (c % 2) * TSW, TSW)
            prog.op("vector", lambda e, c2=c, xs=x1sc: e.tensor_sub(
                xs, bsw(c2), pbank[0].ap()), wait=w)
            prog.op("vector", lambda e, xs=x1sc: e.tensor_mul(xs, xs,
                                                              pbank[1].ap()), inc=[(s_v, 1)])
            vv = s_v.n
            prog.op("scalar", lambda e, x2=xhc, xs=x1sc: e.copy(x2, xs),
                    wait=[(s_v, vv)], inc=[(s_s, 1)])
            ss2 = s_s.n
            prog.op("vector", lambda e, x2=xlc, xh2=xhc, xs=x1sc: e.tensor_sub(
                x2, xs, xh2), wait=[(s_s, ss2)], inc=[(s_v, 1)])
        x1_done[sw] = s_v.n
        x1_done_s = s_s.n
        pbank[0].free = [(s_v, s_v.n)]
        pbank[1].free = [(s_v, s_v.n)]
        b_cons["t"] = s_t.n
        b_cons["s"] = s_s.n

    while split_jobs:
        emit_split_job()
    attn_all = ([(s_t, s_t.n), (s_v, s_v.n), (s_s, s_s.n)]
                + [(s_wspl[i], s_wspl[i].n) for i in range(NSLOT)])
    for _sl in range(NSLOT):
        ring_free[_sl] = s_t.n
        ring_wsp_rd[_sl] = s_wspl[_sl].n
    if debug == "x1":
        for c in range(NCK):
            prog.op("gpsimd", lambda e, c2=c: e.dma_start(
                out=dbg_ap[c2 * 128:(c2 + 1) * 128, :], in_=x1h[:, c2 * L:(c2 + 1) * L].bitcast(F32)[:, 0:L]) if False else e.dma_start(
                out=dbg_ap[c2 * 128:(c2 + 1) * 128, :], in_=x1h[:, c2 * L:(c2 + 1) * L]),
                wait=(attn_all if c == 0 else []), inc=[(s_dbg, 16)], drain=False)

    def comb_ap(p0, pn, c0, cn):
        return A(o_comb, p0, pn, c0, cn)

    prog.op("vector", lambda e: e.tensor_copy(gwh[:, :], gwsb[:, :]),
            wait=[(s_ldc, ld_const)])
    prog.op("vector", lambda e: e.tensor_sub(gwl[:, :], gwsb[:, :], gwh[:, :]),
            inc=[(s_v, 1)])
    gw_split_v = s_v.n
    for tt in range(SW):
        t0g = tt * TSW
        b0, b1 = pbank[0], pbank[1]
        t_lg = 0
        for i in range(NCK):
            w = (list(attn_all) + [(s_v, gw_split_v)]) if (tt == 0 and i == 0) else []
            xh2 = x1h[:, i * L + t0g: i * L + t0g + TSW]
            xl2 = x1l[:, i * L + t0g: i * L + t0g + TSW]
            mm(b0, gwh[:, i * E:(i + 1) * E], xh2, i == 0, False, wait=w, pn=E)
            mm(b0, gwh[:, i * E:(i + 1) * E], xl2, False, False, pn=E)
            mm(b0, gwl[:, i * E:(i + 1) * E], xh2, False, False, pn=E)
            t_lg = mm(b0, gwl[:, i * E:(i + 1) * E], xl2, False, i == NCK - 1, pn=E)
        sc = A(o_sgh, 0, 8, 0, TSW)
        ev("scalar", lambda e: e.activation(sc, b0.ap(0, 8, 0, TSW), AF.Sigmoid),
           bank=b0, wait=[(s_t, t_lg)])
        s_sc = s_s.n
        for q in range(TSW // 128):
            tb = A(o_t8, 0, 128, (tt * 4 + q) * E, E)
            t_tr = tr(b1, sc[:8, q * 128:(q + 1) * 128], c_sb["i128"][:8, :8],
                      wait=[(s_s, s_sc)])
            ev("vector", lambda e, t3=tb: e.tensor_copy(t3, b1.ap(0, 128, 0, E)),
               bank=b1, wait=[(s_t, t_tr)])
            m1 = A(o_r8, 0, 128, 0, 1)
            i1 = A(o_r8, 0, 128, 1, 1)
            m2 = A(o_r8, 0, 128, 2, 1)
            den = A(o_r8, 0, 128, 3, 1)
            mk1 = A(o_r8, 0, 128, 8, E)
            mk2 = A(o_r8, 0, 128, 16, E)
            s2t = A(o_r8, 0, 128, 24, E)
            V = "vector"
            prog.op(V, lambda e, t3=tb: e.tensor_reduce(m1, t3, AX.X, ALU.max))
            prog.op(V, lambda e, t3=tb: e.tensor_scalar(mk1, t3, m1, None, ALU.is_ge))
            prog.op(V, lambda e: e.tensor_mul(mk1, mk1, c_sb["revidx8"][:, :]))
            prog.op(V, lambda e: e.tensor_reduce(i1, mk1, AX.X, ALU.max))
            prog.op(V, lambda e: e.tensor_scalar(i1, i1, -8.0, -1.0, ALU.add, ALU.mult))
            prog.op(V, lambda e: e.tensor_scalar(mk1, c_sb["idxrow8"][:, :], i1,
                                                 None, ALU.is_equal))
            prog.op(V, lambda e, t3=tb: e.scalar_tensor_tensor(s2t, mk1, -1e30, t3,
                                                               ALU.mult, ALU.add))
            prog.op(V, lambda e: e.tensor_reduce(m2, s2t, AX.X, ALU.max))
            prog.op(V, lambda e: e.tensor_scalar(mk2, s2t, m2, None, ALU.is_ge))
            prog.op(V, lambda e: e.tensor_mul(mk2, mk2, c_sb["revidx8"][:, :]))
            prog.op(V, lambda e: e.tensor_reduce(i1, mk2, AX.X, ALU.max))
            prog.op(V, lambda e: e.tensor_scalar(i1, i1, -8.0, -1.0, ALU.add, ALU.mult))
            prog.op(V, lambda e: e.tensor_scalar(mk2, c_sb["idxrow8"][:, :], i1,
                                                 None, ALU.is_equal))
            prog.op(V, lambda e: e.tensor_add(den, m1, m2))
            prog.op(V, lambda e: e.tensor_scalar_add(den, den, 1e-20))
            prog.op(V, lambda e: e.reciprocal(den, den))
            prog.op(V, lambda e: e.tensor_scalar_mul(den, den, TOPK_SCALE))
            prog.op(V, lambda e: e.tensor_mul(m1, m1, den))
            prog.op(V, lambda e: e.tensor_mul(m2, m2, den))
            prog.op(V, lambda e: e.tensor_scalar(mk1, mk1, m1, None, ALU.mult))
            prog.op(V, lambda e, t3=tb: e.scalar_tensor_tensor(t3, mk2, m2, mk1,
                                                               ALU.mult, ALU.add),
                    inc=[(s_v, 1)])
            v_cb = s_v.n
            t_cb = tr(b1, tb, I128, wait=[(s_v, v_cb)], pn=8)
            ev("vector", lambda e, q2=q, t2=t0g: e.tensor_copy(
                comb_ap(0, 8, t2 + q2 * 128, 128), b1.ap(0, 8, 0, 128)),
               bank=b1, wait=[(s_t, t_cb)])
    comb_v = s_v.n

    sgh_rd_t = [0]
    acc_out_v = [0]

    for tt in range(SW):
        if SKIP_FFN:
            break
        t0g = tt * TSW

        def x1ap(i, t2=t0g):
            return (x1h[:, i * L + t2: i * L + t2 + TSW],
                    x1l[:, i * L + t2: i * L + t2 + TSW])

        def accap(c):
            return A(o_acc, 0, 128, c * TSW, TSW)

        def sgh(c):
            return A(o_sgh, 0, 128, c * TSW, TSW)

        def ev_g(oc, bank, tl):
            ev("scalar", lambda e, o2=oc, b2=bank: e.activation(
                sgh(o2), b2.ap(), AF.Silu), bank=bank,
               wait=[(s_t, tl), (s_t, sgh_rd_t[0])])
        proj3((ws_gTh, ws_gTl), 8, 8, x1ap, [pbank[0], pbank[1]], ev_g,
              wait_first=[(s_v, comb_v)])
        sg_s = s_s.n
        def hhap(oc):
            return (hhh[:, oc * TSW:(oc + 1) * TSW], hhl[:, oc * TSW:(oc + 1) * TSW])

        def ev_u(oc, bank, tl):
            ev("vector", lambda e, o2=oc, b2=bank: e.tensor_mul(
                sgh(o2), sgh(o2), b2.ap()), bank=bank,
               wait=[(s_t, tl), (s_s, sg_s)])
            vv = s_v.n
            prog.op("scalar", lambda e, o2=oc: e.copy(hhap(o2)[0], sgh(o2)),
                    wait=[(s_v, vv), (s_t, sgh_rd_t[0])], inc=[(s_s, 1)])
            ss2 = s_s.n
            prog.op("vector", lambda e, o2=oc: e.tensor_sub(
                hhap(o2)[1], sgh(o2), hhap(o2)[0]), wait=[(s_s, ss2)], inc=[(s_v, 1)])
        proj3((ws_uTh, ws_uTl), 8, 8, x1ap, [pbank[2], pbank[3]], ev_u)
        hh_v = s_v.n
        def ev_d(oc, bank, tl):
            xa = x1ap(oc)
            prog.op("vector", lambda e, o2=oc, x2=xa: e.tensor_add(
                accap(o2), x2[0], x2[1]),
                wait=[(s_t, tl), (s_v, acc_out_v[0])])
            ev("vector", lambda e, o2=oc, b2=bank: e.tensor_add(
                accap(o2), accap(o2), b2.ap()), bank=bank)
        proj3((ws_dTh, ws_dTl), 8, 8, hhap, [pbank[4], pbank[5]], ev_d,
              wait_first=[(s_v, hh_v)])
        sgh_rd_t[0] = s_t.n

        for e_i in range(E):
            b7 = pbank[7]
            t_cbb = mm(b7, c_sb["esel"][:, e_i * 128:(e_i + 1) * 128],
                       comb_ap(0, 8, t0g, TSW), True, True,
                       wait=[(s_v, comb_v)])

            def ev_ge(oc, bank, tl):
                ev("scalar", lambda e, o2=oc, b2=bank: e.activation(
                    sgh(o2), b2.ap(), AF.Silu), bank=bank,
                   wait=[(s_t, tl), (s_t, sgh_rd_t[0])])
            proj3((w_gTh[e_i * D:(e_i + 1) * D, :], w_gTl[e_i * D:(e_i + 1) * D, :]), 4, 8, x1ap,
                  [pbank[0], pbank[1]], ev_ge)
            sg_s2 = s_s.n

            def ev_ue(oc, bank, tl):
                ev("vector", lambda e, o2=oc, b2=bank: e.tensor_mul(
                    sgh(o2), sgh(o2), b2.ap()), bank=bank,
                   wait=[(s_t, tl), (s_s, sg_s2), (s_t, t_cbb)])
                prog.op("vector", lambda e, o2=oc: e.tensor_mul(
                    sgh(o2), sgh(o2), b7.ap()), inc=[(s_v, 1)])
                vv = s_v.n
                prog.op("scalar", lambda e, o2=oc: e.copy(hhap(o2)[0], sgh(o2)),
                        wait=[(s_v, vv), (s_t, sgh_rd_t[0])], inc=[(s_s, 1)])
                ss2 = s_s.n
                prog.op("vector", lambda e, o2=oc: e.tensor_sub(
                    hhap(o2)[1], sgh(o2), hhap(o2)[0]), wait=[(s_s, ss2)],
                    inc=[(s_v, 1)])
            proj3((w_uTh[e_i * D:(e_i + 1) * D, :], w_uTl[e_i * D:(e_i + 1) * D, :]), 4, 8, x1ap,
                  [pbank[2], pbank[3]], ev_ue)
            hh_v2 = s_v.n
            b7.free = [(s_v, hh_v2)]

            def ev_de(oc, bank, tl):
                ev("vector", lambda e, o2=oc, b2=bank: e.tensor_add(
                    accap(o2), accap(o2), b2.ap()), bank=bank, wait=[(s_t, tl)])
            proj3((w_dTh[e_i * IE:(e_i + 1) * IE, :], w_dTl[e_i * IE:(e_i + 1) * IE, :]), 8, 4, hhap,
                  [pbank[4], pbank[5]], ev_de, wait_first=[(s_v, hh_v2)])
            sgh_rd_t[0] = s_t.n
        routed_v = s_v.n

        v_st3 = ln_stats(lambda c: accap(c), MU, RR, [(s_v, routed_v)],
                         pbank[0], pbank[1])
        mm(pbank[0], ONESR, MU, True, True, wait=[(s_v, v_st3)])
        t_bc4 = mm(pbank[1], ONESR, RR, True, True)
        ost = A(o_outst, 0, 128, 0, TSW)
        for c in range(NCK):
            w = [(s_t, t_bc4)] if c == 0 else []
            w = w + [(s_out, 16 * (tt * NCK + c))]
            prog.op("vector", lambda e, c2=c, o2=ost: e.tensor_sub(
                o2, accap(c2), pbank[0].ap()), wait=w)
            prog.op("vector", lambda e, o2=ost: e.tensor_mul(o2, o2, pbank[1].ap()),
                    inc=[(s_v, 1)])
            v_o = s_v.n
            prog.op("gpsimd", lambda e, c2=c, o2=ost, t2=t0g: e.dma_start(
                out=yt[c2 * 128:(c2 + 1) * 128, t2:t2 + TSW], in_=o2),
                wait=[(s_v, v_o)], inc=[(s_out, 16)], drain=False)
        pbank[0].free = [(s_v, s_v.n)]
        pbank[1].free = [(s_v, s_v.n)]
        acc_out_v[0] = s_v.n

    if debug == "comb":
        prog.op("gpsimd", lambda e: e.dma_start(out=dbg_ap, in_=comb_ap(0, 8, 0, L)),
                wait=[(s_v, comb_v)], inc=[(s_dbg, 16)], drain=False)

    if SKIP_FFN:
        for c in range(NCK):
            prog.op("gpsimd", lambda e, c2=c: e.dma_start(
                out=yt[c2 * 128:(c2 + 1) * 128, :], in_=x1f[:, c2 * L:(c2 + 1) * L]),
                wait=(attn_all if c == 0 else []), inc=[(s_out, 16)], drain=False)
    prog.op("gpsimd", lambda e: e.nop(),
            wait=[(s_out, s_out.n)] + ([(s_dbg, s_dbg.n)] if s_dbg.n else []))

    with stack:
        with contextlib.ExitStack() as ss2:
            for s2 in sems:
                s2.handle = ss2.enter_context(nc.semaphore(s2.name))
            with nc.Block() as block:
                @block.tensor
                def _(eng):
                    prog.emit("tensor", eng)

                @block.vector
                def _(eng):
                    prog.emit("vector", eng)

                @block.scalar
                def _(eng):
                    prog.emit("scalar", eng)

                @block.gpsimd
                def _(eng):
                    prog.emit("gpsimd", eng)

                @block.sync
                def _(eng):
                    prog.emit("sync", eng)
    return nc


def make_inmaps(L0_attn_x, L0_attn_W, L0_attn_k_norm_w, L0_attn_o_norm_w,
                L0_attn_o_norm_b, L0_attn_eta_raw, L0_attn_lam_logit,
                L0_ffn_gate_weight, L0_ffn_w_gate, L0_ffn_w_up, L0_ffn_w_down,
                L0_ffn_ws_gate, L0_ffn_ws_up, L0_ffn_ws_down):
    f32 = np.float32
    ac = np.ascontiguousarray
    cc = host_constants()
    shared = {
        "wT": ac(np.asarray(L0_attn_W, f32).T),
        "ws_gT": ac(np.asarray(L0_ffn_ws_gate, f32).T),
        "ws_uT": ac(np.asarray(L0_ffn_ws_up, f32).T),
        "ws_dT": ac(np.asarray(L0_ffn_ws_down, f32).T),
        "w_gT": ac(np.transpose(np.asarray(L0_ffn_w_gate, f32), (0, 2, 1))).reshape(E * D, IE),
        "w_uT": ac(np.transpose(np.asarray(L0_ffn_w_up, f32), (0, 2, 1))).reshape(E * D, IE),
        "w_dT": ac(np.transpose(np.asarray(L0_ffn_w_down, f32), (0, 2, 1))).reshape(E * IE, D),
        "gwT": ac(np.asarray(L0_ffn_gate_weight, f32).T),
        "onwT": ac(np.asarray(L0_attn_o_norm_w, f32).reshape(NCK, 128).T),
        "onbT": ac(np.asarray(L0_attn_o_norm_b, f32).reshape(NCK, 128).T),
        "knw": ac(np.tile(np.asarray(L0_attn_k_norm_w, f32), 2)[:, None]),
        "eta_row": ac(np.asarray(L0_attn_eta_raw, f32)[None, :]),
        "laml_row": ac(np.asarray(L0_attn_lam_logit, f32)[None, :]),
    }
    for n, a in cc.items():
        shared[n] = ac(a, dtype=f32)
    in_maps = []
    for b in range(8):
        m = dict(shared)
        m["xt"] = ac(np.asarray(L0_attn_x[b], f32).T)
        in_maps.append(m)
    return in_maps


def kernel(**inputs):
    nc = build_program()
    in_maps = make_inmaps(**inputs)
    res = run_bass_kernel_spmd(nc, in_maps, core_ids=list(range(8)))
    out = np.empty((8, L, D), dtype=np.float32)
    for b in range(8):
        out[b] = res.results[b]["yt"].T
    return out


# version 14
# speedup vs baseline: 1.1573x; 1.1573x over previous
import contextlib

import numpy as np

import concourse.bass as bass
import concourse.mybir as mybir
from concourse.bass_utils import run_bass_kernel_spmd

F32 = mybir.dt.float32
BF16 = mybir.dt.bfloat16
AF = mybir.ActivationFunctionType
ALU = mybir.AluOpType
AX = mybir.AxisListType

D = 1024
L = 2048
H = 16
HD = 64
NCK = 8
SW = 4
TSW = 512
CH = 128
E = 8
IE = 512
SHI = 1024
LN_EPS = 1e-5
RMS_EPS = float(np.finfo(np.float32).eps)
TOPK_SCALE = 2.5
NSLOT = 4
SKIP_DELTA = False
DELTA_SW = None
SKIP_FFN = False


class Sem:
    def __init__(self, name):
        self.name, self.n, self.handle = name, 0, None


class Prog:
    ENGINES = ("tensor", "vector", "scalar", "gpsimd", "sync")

    def __init__(self):
        self.ops = {e: [] for e in self.ENGINES}

    def op(self, engine, fn, wait=(), inc=(), drain=None):
        if drain is None:
            drain = engine in ("vector", "scalar", "gpsimd")
        w = [(s, c) for (s, c) in wait if c > 0]
        for s, amt in inc:
            s.n += amt
        self.ops[engine].append((tuple(w), fn, tuple(inc), drain))

    def emit(self, engine_name, eng):
        first = True
        for waits, fn, incs, drain in self.ops[engine_name]:
            if drain and not first:
                eng.drain()
            for s, c in waits:
                eng.wait_ge(s.handle, c)
            ins = fn(eng)
            for s, amt in incs:
                ins = ins.then_inc(s.handle, amt)
            first = False


class Bank:
    def __init__(self, t):
        self.t = t
        self.free = []

    def ap(self, p0=0, pn=128, c0=0, cn=512):
        return self.t[p0:p0 + pn, c0:c0 + cn]


def host_constants():
    c = {}
    c["i128"] = np.eye(128, dtype=np.float32)
    tau = np.arange(1, CH + 1)
    dif = (tau[:, None] - tau[None, :]).astype(np.float32)
    c["iota_l"] = np.where(dif > 0, dif, 0).astype(np.float32)
    c["iota_u"] = np.ascontiguousarray(c["iota_l"].T)
    c["u01s"] = (c["iota_u"] > 0).astype(np.float32)
    c["u01i"] = (dif.T >= 0).astype(np.float32)
    c["iota_row"] = np.broadcast_to(tau.astype(np.float32)[None, :], (128, 128)).copy()
    c["iota_revcol"] = (CH - tau).astype(np.float32)[:, None].copy()
    c["ones_row"] = np.ones((1, 128), dtype=np.float32)
    c["ones_col"] = np.ones((128, 1), dtype=np.float32)
    hm = np.zeros((NCK, 128, H), dtype=np.float32)
    hs = np.zeros((NCK, H, 128), dtype=np.float32)
    for cc in range(NCK):
        for p in range(128):
            h = 2 * cc + p // 64
            hm[cc, p, h] = 1.0
            hs[cc, h, p] = 1.0
    c["headmask"] = np.ascontiguousarray(hm.transpose(1, 0, 2).reshape(128, NCK * H))
    c["headsel"] = np.ascontiguousarray(hs.transpose(1, 0, 2).reshape(H, NCK * 128))
    c["idxrow8"] = np.broadcast_to(np.arange(E, dtype=np.float32)[None, :], (128, E)).copy()
    c["revidx8"] = (float(E) - c["idxrow8"]).copy()
    es = np.zeros((E, E * 128), dtype=np.float32)
    for e in range(E):
        es[e, e * 128:(e + 1) * 128] = 1.0
    c["esel"] = es
    return c


CONST_SHAPES = {
    "i128": [128, 128], "iota_l": [128, 128], "iota_u": [128, 128],
    "u01s": [128, 128], "u01i": [128, 128], "iota_row": [128, 128],
    "iota_revcol": [128, 1], "ones_row": [1, 128], "ones_col": [128, 1],
    "headmask": [128, NCK * H], "headsel": [H, NCK * 128],
    "idxrow8": [128, E], "revidx8": [128, E], "esel": [E, E * 128],
}


def build_program(debug=None):
    nc = bass.Bass("TRN2", target_bir_lowering=False)
    dt = nc.dram_tensor

    xt = dt("xt", [D, L], F32, kind="ExternalInput").ap()
    wT = dt("wT", [D, D], F32, kind="ExternalInput").ap()
    ws_gT = dt("ws_gT", [D, SHI], F32, kind="ExternalInput").ap()
    ws_uT = dt("ws_uT", [D, SHI], F32, kind="ExternalInput").ap()
    ws_dT = dt("ws_dT", [SHI, D], F32, kind="ExternalInput").ap()
    w_gT = dt("w_gT", [E * D, IE], F32, kind="ExternalInput").ap()
    w_uT = dt("w_uT", [E * D, IE], F32, kind="ExternalInput").ap()
    w_dT = dt("w_dT", [E * IE, D], F32, kind="ExternalInput").ap()
    gwT = dt("gwT", [D, E], F32, kind="ExternalInput").ap()
    onwT = dt("onwT", [128, NCK], F32, kind="ExternalInput").ap()
    onbT = dt("onbT", [128, NCK], F32, kind="ExternalInput").ap()
    knw_d = dt("knw", [128, 1], F32, kind="ExternalInput").ap()
    eta_d = dt("eta_row", [1, H], F32, kind="ExternalInput").ap()
    laml_d = dt("laml_row", [1, H], F32, kind="ExternalInput").ap()
    cst = {n: dt(n, s, F32, kind="ExternalInput").ap() for n, s in CONST_SHAPES.items()}
    ws_gTh = dt("ws_gTh", [D, SHI], BF16).ap()
    ws_gTl = dt("ws_gTl", [D, SHI], BF16).ap()
    ws_uTh = dt("ws_uTh", [D, SHI], BF16).ap()
    ws_uTl = dt("ws_uTl", [D, SHI], BF16).ap()
    ws_dTh = dt("ws_dTh", [SHI, D], BF16).ap()
    ws_dTl = dt("ws_dTl", [SHI, D], BF16).ap()
    w_gTh = dt("w_gTh", [E * D, IE], BF16).ap()
    w_gTl = dt("w_gTl", [E * D, IE], BF16).ap()
    w_uTh = dt("w_uTh", [E * D, IE], BF16).ap()
    w_uTl = dt("w_uTl", [E * D, IE], BF16).ap()
    w_dTh = dt("w_dTh", [E * IE, D], BF16).ap()
    w_dTl = dt("w_dTl", [E * IE, D], BF16).ap()
    yt = dt("yt", [D, L], F32, kind="ExternalOutput").ap()
    dbg_ap = None
    if debug in ("static", "k", "attnpre", "x1"):
        dbg_ap = dt("dbg", [D, L], F32, kind="ExternalOutput").ap()
    elif debug == "comb":
        dbg_ap = dt("dbg", [E, L], F32, kind="ExternalOutput").ap()

    prog = Prog()
    s_ldc = Sem("ldc")
    s_ldx = Sem("ldx")
    s_sl = [Sem(f"sl{i}") for i in range(NSLOT)]
    s_t = Sem("t")
    s_v = Sem("v")
    s_s = Sem("s")
    s_out = Sem("out")
    s_dbg = Sem("dbg")
    s_g = Sem("g")
    s_gc = Sem("gc")
    s_wsp = Sem("wsp")
    s_wspl = [Sem(f"wspl{i}") for i in range(NSLOT)]
    sems = [s_ldc, s_ldx, s_t, s_v, s_s, s_out, s_dbg, s_g, s_gc, s_wsp] + s_sl + s_wspl

    stack = contextlib.ExitStack()

    def sb(name, shape, dtype=F32):
        return stack.enter_context(nc.sbuf_tensor(name, shape, dtype))

    c_sb = {n: sb("c_" + n, CONST_SHAPES[n]) for n in CONST_SHAPES}
    c_onw = sb("c_onw", [128, NCK])
    c_onb = sb("c_onb", [128, NCK])
    c_knw = sb("c_knw", [128, 1])
    c_eps = sb("c_eps", [128, 2])
    hrow = sb("hrow", [1, H * 6])
    ph = sb("ph", [128, H * 3])
    strow = sb("strow", [1, 3 * TSW])
    st16 = sb("st16", [16, 2 * TSW])
    sqt = sb("sqt", [128, 2 * TSW])
    gwsb = sb("gwsb", [128, NCK * E])
    gwh = sb("gwh", [128, NCK * E], BF16)
    gwl = sb("gwl", [128, NCK * E], BF16)
    x1h = sb("x1h", [128, NCK * L], BF16)
    x1l = sb("x1l", [128, NCK * L], BF16)
    ring = sb("ring", [128, NSLOT * 1024])

    AR = 23936
    arena = sb("arena", [128, AR])

    class Off:
        def __init__(self):
            self.o = 0

        def take(self, n):
            r = self.o
            self.o += n
            assert self.o <= AR, f"arena overflow {self.o}"
            return r

    att = Off()
    o_xsw = att.take(NCK * TSW)
    o_bsw = att.take(NCK * TSW)
    o_ksw = att.take(NCK * TSW)
    o_maskY = att.take(H * CH)
    o_masknA = att.take(H * CH)
    o_lb = att.take(H * CH)
    o_nblr = att.take(H)
    o_sstate = att.take(H * HD)
    o_slot = []
    for s_i in range(2):
        d = {"ypow": att.take(5 * CH), "yt": att.take(2 * CH), "nale": att.take(CH),
             "dkf": att.take(CH), "nbk": att.take(HD), "ktm": att.take(HD)}
        o_slot.append(d)
    o_xchp = att.take(2 * CH)
    o_rtp = att.take(CH)
    o_otmp = att.take(CH)
    o_x1s = att.take(2 * TSW)

    ffn = Off()
    o_sgh = ffn.take(NCK * TSW)
    o_acc = ffn.take(NCK * TSW)
    o_comb = ffn.take(L)
    o_t8 = ffn.take(16 * E)
    o_r8 = ffn.take(64)
    o_outst = ffn.take(TSW)
    o_hhh = ffn.take(NCK * TSW // 2)
    o_hhl = ffn.take(NCK * TSW // 2)
    o_ringh = ffn.take(NSLOT * 512)
    o_ringl = ffn.take(NSLOT * 512)

    def A(off, p0, pn, c0, cn):
        return arena[p0:p0 + pn, off + c0:off + c0 + cn]

    def AB(off, f32cols):
        return arena[:, off:off + f32cols].bitcast(BF16)



    ring_h = AB(o_ringh, NSLOT * 512)
    ring_l = AB(o_ringl, NSLOT * 512)
    hhh = AB(o_hhh, NCK * TSW // 2)
    hhl = AB(o_hhl, NCK * TSW // 2)

    pbank = [Bank(stack.enter_context(nc.psum_tensor(f"pb{i}", [128, 512], F32)))
             for i in range(8)]

    def mm(bank, lhsT, rhs, start, stop, wait=(), p0=0, pn=128, c0=0, cn=None):
        cn = cn if cn is not None else rhs.shape[-1]
        w = list(wait)
        if start:
            w += bank.free
            bank.free = []
        prog.op("tensor",
                lambda e, b=bank, l2=lhsT, r2=rhs, st=start, sp=stop, q=(p0, pn, c0, cn):
                e.matmul(b.ap(*q), l2, r2, start=st, stop=sp),
                wait=w, inc=[(s_t, 1)])
        return s_t.n

    def tr(bank, in_, ident, wait=(), p0=0, pn=None, c0=0, cn=None, start=True,
           stop=True):
        pn = pn if pn is not None else in_.shape[-1]
        cn = cn if cn is not None else in_.shape[0]
        w = list(wait)
        if start:
            w += bank.free
            bank.free = []
        prog.op("tensor",
                lambda e, b=bank, i2=in_, id2=ident, q=(p0, pn, c0, cn), st=start, sp=stop:
                e.matmul(b.ap(*q), i2, id2, start=st, stop=sp, is_transpose=True),
                wait=w, inc=[(s_t, 1)])
        return s_t.n

    def ev(engine, fn, bank=None, wait=()):
        sem = s_v if engine == "vector" else s_s
        prog.op(engine, fn, wait=wait, inc=[(sem, 1)])
        if bank is not None:
            bank.free.append((sem, sem.n))
        return sem.n

    def dma(engine, out_ap, in_ap, wait=(), sem=None):
        sem = sem or s_ldc
        prog.op(engine, lambda e, o=out_ap, i=in_ap: e.dma_start(out=o, in_=i),
                wait=wait, inc=[(sem, 16)], drain=False)
        return sem.n

    sq_rd = [0, 0]

    for n in CONST_SHAPES:
        dma("sync", c_sb[n][:, :], cst[n])
    dma("sync", c_onw[:, :], onwT)
    dma("sync", c_onb[:, :], onbT)
    dma("sync", c_knw[:, :], knw_d)
    dma("sync", hrow[:1, 0:H], eta_d)
    dma("sync", hrow[:1, H:2 * H], laml_d)
    dma("sync", gwsb[:, :].rearrange("p (c w) -> p c w", c=NCK),
        gwT.rearrange("(c p) w -> p c w", p=128))
    ld_const = s_ldc.n

    def hr(i):
        return hrow[:1, i * H:(i + 1) * H]

    I128 = c_sb["i128"][:, :]
    I64 = c_sb["i128"][:64, :64]

    def ident64(pr):
        return c_sb["i128"][pr:pr + 64, pr:pr + 64]
    ONESR = c_sb["ones_row"][:, :]
    ONESC = c_sb["ones_col"][:, :]

    prog.op("vector", lambda e: e.memset(c_eps[:, 0:1], RMS_EPS))
    prog.op("vector", lambda e: e.memset(c_eps[:, 1:2], LN_EPS))
    prog.op("scalar", lambda e: e.activation(hr(0), hr(0), AF.Tanh),
            wait=[(s_ldc, ld_const)])
    prog.op("scalar", lambda e: e.activation(hr(1), hr(1), AF.Sigmoid))
    prog.op("scalar", lambda e: e.activation(hr(3), hr(1), AF.Ln), inc=[(s_s, 1)])
    sw0 = s_s.n
    prog.op("scalar", lambda e: e.activation(hr(4), hr(3), AF.Exp, scale=float(CH)),
            inc=[(s_s, 1)])
    prog.op("vector", lambda e: e.tensor_tensor(hr(2), hr(0), hr(1), ALU.mult),
            wait=[(s_s, sw0)])
    prog.op("vector", lambda e: e.tensor_sub(hr(2), hr(2), hr(0)))
    prog.op("vector", lambda e: e.tensor_scalar_mul(hr(2), hr(2), 1.0 / HD))
    prog.op("vector", lambda e: e.tensor_scalar_mul(hr(5), hr(2), -1.0), inc=[(s_v, 1)])
    v_hr, s_hr = s_v.n, s_s.n
    b0 = pbank[0]
    mm(b0, ONESR, hr(3), True, False, wait=[(s_v, v_hr), (s_s, s_hr)], c0=0, cn=H)
    mm(b0, ONESR, hr(5), False, False, c0=H, cn=H)
    t_ph = mm(b0, ONESR, hr(4), False, True, c0=2 * H, cn=H)
    ev("vector", lambda e: e.tensor_copy(ph[:, :], b0.ap(0, 128, 0, 3 * H)),
       bank=b0, wait=[(s_t, t_ph)])
    v_ph = s_v.n

    def PH(i, h):
        return ph[:, i * H + h:i * H + h + 1]

    for h in range(H):
        mY = A(o_maskY, 0, 128, h * CH, CH)
        mA_ = A(o_masknA, 0, 128, h * CH, CH)
        lbh = A(o_lb, 0, 64, h * CH, CH)
        nbh = A(o_nblr, 0, 128, h, 1)
        prog.op("scalar", lambda e, o=mY, s1=PH(0, h): e.activation(
            o, c_sb["iota_u"][:, :], AF.Exp, scale=s1),
            wait=[(s_v, v_ph)], inc=[(s_s, 1)])
        sw1 = s_s.n
        prog.op("vector", lambda e, o=mA_, i=mY, s1=PH(1, h): e.scalar_tensor_tensor(
            o, i, s1, c_sb["u01i"][:, :], ALU.mult, ALU.mult),
            wait=[(s_s, sw1)])
        prog.op("vector", lambda e, o=mY, s1=PH(1, h): e.scalar_tensor_tensor(
            o, o, s1, c_sb["u01s"][:, :], ALU.mult, ALU.mult), inc=[(s_v, 1)])
        prog.op("scalar", lambda e, o=lbh, s1=PH(0, h): e.activation(
            o, c_sb["iota_row"][:64, :], AF.Exp, scale=s1[:64, :]),
            inc=[(s_s, 1)])
        prog.op("scalar", lambda e, o=nbh, s1=PH(0, h): e.activation(
            o, c_sb["iota_revcol"][:, :], AF.Exp, scale=s1), inc=[(s_s, 1)])
        sw2 = s_s.n
        prog.op("vector", lambda e, o=nbh, s1=PH(1, h): e.tensor_scalar(
            o, o, s1, None, ALU.mult), wait=[(s_s, sw2)])
        prog.op("vector", lambda e, o=A(o_sstate, 0, 64, h * HD, HD): e.memset(o, 0.0),
                inc=[(s_v, 1)])
    setup_v, setup_s = s_v.n, s_s.n

    ring_free = [0] * NSLOT
    rsl = [0]

    def slab_cols(sl, i):
        return ring[:, sl * 1024 + i * 128: sl * 1024 + (i + 1) * 128]

    ring_f32_rd = [0] * NSLOT
    gsub_flip = [0]
    split_jobs = []

    def make_split_jobs():
        mats = [(ws_gT, ws_gTh, ws_gTl, D, SHI), (ws_uT, ws_uTh, ws_uTl, D, SHI),
                (ws_dT, ws_dTh, ws_dTl, SHI, D), (w_gT, w_gTh, w_gTl, E * D, IE),
                (w_uT, w_uTh, w_uTl, E * D, IE), (w_dT, w_dTh, w_dTl, E * IE, D)]
        for mf, mh, ml, rows, cols in mats:
            for r0 in range(0, rows, 128):
                split_jobs.append((mf, mh, ml, r0, cols))

    def emit_split_job():
        if not split_jobs:
            return
        mf, mh, ml, r0, cols = split_jobs.pop(0)
        sl = rsl[0] % NSLOT
        rsl[0] += 1
        slf = ring[:, sl * 1024: sl * 1024 + cols]
        dma("sync", slf, mf[r0:r0 + 128, :], wait=[(s_t, ring_free[sl])],
            sem=s_sl[sl])
        ld = s_sl[sl].n
        sl2 = rsl[0] % NSLOT
        rsl[0] += 1
        stage = ring[:, sl2 * 1024: sl2 * 1024 + 1024].bitcast(BF16)
        sh2 = stage[:, 0:cols]
        sl3 = stage[:, 1024:1024 + cols]
        prog.op("scalar", lambda e, o=sh2, i2=slf: e.copy(o, i2),
                wait=[(s_sl[sl], ld), (s_t, ring_free[sl2]),
                      (s_wspl[sl2], ring_wsp_rd[sl2])], inc=[(s_s, 1)])
        ssp = s_s.n
        prog.op("vector", lambda e, o=sl3, a=slf, b=sh2: e.tensor_sub(o, a, b),
                wait=[(s_s, ssp)], inc=[(s_v, 1)])
        vv = s_v.n
        ring_free_f = s_v.n
        dma("sync", mh[r0:r0 + 128, :], sh2, wait=[(s_v, vv)], sem=s_wspl[sl2])
        dma("sync", ml[r0:r0 + 128, :], sl3, sem=s_wspl[sl2])
        ring_wsp_rd[sl2] = s_wspl[sl2].n
        ring_f32_rd[sl] = 0
        ring_vrd[sl] = ring_free_f

    ring_wsp_rd = [0] * NSLOT
    ring_vrd = [0] * NSLOT

    def proj3(mat_hl, n_out, kdim, rhs_fn, banks, evac, wait_first=()):
        math_ap, matl_ap = mat_hl
        for oc in range(n_out):
            sl = rsl[0] % NSLOT
            rsl[0] += 1
            view = ring[:, sl * 1024: sl * 1024 + 1024].bitcast(BF16)
            srch = math_ap[:, oc * 128:(oc + 1) * 128].rearrange(
                "(c p) j -> p c j", p=128)
            srcl = matl_ap[:, oc * 128:(oc + 1) * 128].rearrange(
                "(c p) j -> p c j", p=128)
            wfree = [(s_t, ring_free[sl]), (s_v, ring_vrd[sl]),
                     (s_wspl[sl], ring_wsp_rd[sl])]
            dma("sync", view[:, 0:kdim * 128].rearrange("p (c j) -> p c j", c=kdim),
                srch, wait=wfree, sem=s_sl[sl])
            dma("sync", view[:, 1024:1024 + kdim * 128].rearrange(
                "p (c j) -> p c j", c=kdim), srcl, sem=s_sl[sl])
            ld = s_sl[sl].n
            bank = banks[oc % len(banks)]
            t_l = 0
            for i in range(kdim):
                w = [(s_sl[sl], ld)]
                if i == 0:
                    w += list(wait_first)
                xh, xl = rhs_fn(i)
                wh = view[:, i * 128:(i + 1) * 128]
                wl = view[:, 1024 + i * 128:1024 + (i + 1) * 128]
                mm(bank, wh, xh, i == 0, False, wait=w)
                mm(bank, wh, xl, False, False)
                t_l = mm(bank, wl, xh, False, i == kdim - 1)
            ring_free[sl] = t_l
            evac(oc, bank, t_l)

    def proj(mat_ap, n_out, kdim, rhs_fn, banks, evac, wait_first=()):
        for oc in range(n_out):
            sl = rsl[0] % NSLOT
            rsl[0] += 1
            src = mat_ap[:, oc * 128:(oc + 1) * 128].rearrange(
                "(c p) j -> p c j", p=128)
            dst = ring[:, sl * 1024: sl * 1024 + kdim * 128].rearrange(
                "p (c j) -> p c j", c=kdim)
            dma("sync", dst, src, wait=[(s_t, ring_free[sl]), (s_v, ring_vrd[sl]),
                                        (s_wspl[sl], ring_wsp_rd[sl])], sem=s_sl[sl])
            ld = s_sl[sl].n
            bank = banks[oc % len(banks)]
            t_l = 0
            for i in range(kdim):
                w = [(s_sl[sl], ld)]
                if i == 0:
                    w += list(wait_first)
                t_l = mm(bank, slab_cols(sl, i), rhs_fn(i), i == 0, i == kdim - 1,
                         wait=w)
            ring_free[sl] = t_l
            evac(oc, bank, t_l)

    MUSQ = strow[:1, 2 * TSW:3 * TSW]

    def ln_stats(src_fn, mu_ap, r_ap, first_wait, b_sum, b_sq):
        for c in range(NCK):
            w = list(first_wait) if c == 0 else []
            mm(b_sum, ONESC, src_fn(c), c == 0, c == NCK - 1, wait=w, pn=1)
        for c in range(NCK):
            half = c % 2
            sq = sqt[:, half * TSW: half * TSW + TSW]
            w = [(s_t, sq_rd[half])] + (list(first_wait) if c == 0 else [])
            prog.op("scalar", lambda e, s2=sq, c2=c, fn=src_fn: e.activation(
                s2, fn(c2), AF.Square), wait=w, inc=[(s_s, 1)])
            ssq = s_s.n
            sq_rd[half] = mm(b_sq, ONESC, sq, c == 0, c == NCK - 1,
                             wait=[(s_s, ssq)], pn=1)
        t_st = s_t.n
        prog.op("scalar", lambda e, m2=mu_ap: e.activation(
            m2, b_sum.ap(0, 1, 0, TSW), AF.Copy, scale=1.0 / D),
            wait=[(s_t, t_st)], inc=[(s_s, 1)])
        b_sum.free = [(s_s, s_s.n)]
        prog.op("scalar", lambda e, r2=r_ap: e.activation(
            r2, b_sq.ap(0, 1, 0, TSW), AF.Copy, scale=1.0 / D), inc=[(s_s, 1)])
        b_sq.free = [(s_s, s_s.n)]
        s_m = s_s.n
        prog.op("vector", lambda e, m2=mu_ap: e.tensor_tensor(MUSQ, m2, m2, ALU.mult),
                wait=[(s_s, s_m)])
        prog.op("vector", lambda e, r2=r_ap: e.tensor_sub(r2, r2, MUSQ), inc=[(s_v, 1)])
        v_va = s_v.n
        prog.op("scalar", lambda e, r2=r_ap: e.activation(r2, r2, AF.Sqrt,
                                                         bias=c_eps[:1, 1:2]),
                wait=[(s_v, v_va)], inc=[(s_s, 1)])
        s_q = s_s.n
        prog.op("vector", lambda e, r2=r_ap: e.reciprocal(r2, r2),
                wait=[(s_s, s_q)], inc=[(s_v, 1)])
        return s_v.n

    def xsw(c, t0=0, n=TSW, p0=0, pn=128):
        return A(o_xsw, p0, pn, c * TSW + t0, n)

    def bsw(c, t0=0, n=TSW, p0=0, pn=128):
        return A(o_bsw, p0, pn, c * TSW + t0, n)

    def ksw(c, t0=0, n=TSW, p0=0, pn=128):
        return A(o_ksw, p0, pn, c * TSW + t0, n)

    make_split_jobs()
    x_cons = {"t": 0, "s": 0, "v": 0}
    b_cons = {"t": 0, "s": 0}
    head_dep = {h: [] for h in range(H)}
    slot_free_t = [0, 0]
    x1_done = [0] * SW

    def delta_ctx(h, slot):
        cc, pr = h // 2, (h % 2) * 64
        d = o_slot[slot]

        def yp(k2):
            return A(d["ypow"], 0, 128, (k2 % 5) * CH, CH)

        def ytp(k2):
            return A(d["yt"], 0, 128, (k2 % 2) * CH, CH)

        return {"h": h, "slot": slot, "cc": cc, "pr": pr, "yp": yp, "ytp": ytp,
                "nal": A(d["nale"], 0, 128, 0, CH),
                "dkf": A(d["dkf"], 0, 64, 0, CH),
                "nbk": A(d["nbk"], 0, 128, 0, HD),
                "ktm": A(d["ktm"], 0, 128, 0, HD),
                "mY": A(o_maskY, 0, 128, h * CH, CH),
                "mA": A(o_masknA, 0, 128, h * CH, CH),
                "lbh": A(o_lb, 0, 64, h * CH, CH),
                "nbh": A(o_nblr, 0, 128, h, 1),
                "pb_a": pbank[2 + 2 * slot], "pb_t": pbank[3 + 2 * slot]}

    def delta_pair_emit(ca, cb, ci, k_dep, static_dep):
        h0, h1 = ca["h"], cb["h"]
        cc = ca["cc"]
        pb_q, pb_s = pbank[6], pbank[7]
        bpair = bsw(cc, ci * CH, CH)
        rtp = A(o_rtp, 0, 128, 0, CH)
        otp = A(o_otmp, 0, 128, 0, CH)

        def xch(i):
            return A(o_xchp, 0, 128, (i % 2) * CH, CH)

        sst0 = A(o_sstate, 0, 64, h0 * HD, HD)
        sst1 = A(o_sstate, 0, 64, h1 * HD, HD)
        sstp = A(o_sstate, 0, 64, h0 * HD, 2 * HD)

        gate = [(s_t, slot_free_t[0]), (s_t, slot_free_t[1])]
        kf = {}
        t_a = {}
        for cx in (ca, cb):
            kf[cx["h"]] = ksw(cx["cc"], ci * CH, CH, cx["pr"], 64)
            t_a[cx["h"]] = mm(cx["pb_a"], kf[cx["h"]], kf[cx["h"]], True, True,
                              wait=k_dep + gate, cn=CH)
        v_y = {}
        for cx in (ca, cb):
            ev("vector", lambda e, c2=cx: e.tensor_tensor(
                c2["yp"](0), c2["pb_a"].ap(0, 128, 0, CH), c2["mY"], ALU.mult),
               wait=[(s_t, t_a[cx["h"]])])
            ev("vector", lambda e, c2=cx: e.tensor_tensor(
                c2["nal"], c2["pb_a"].ap(0, 128, 0, CH), c2["mA"], ALU.mult),
               bank=cx["pb_a"])
            v_y[cx["h"]] = s_v.n
        sqdep = {}
        v_dkf = {}
        v_nbk = {}
        for cx in (ca, cb):
            h = cx["h"]
            t_ktm = tr(cx["pb_t"], kf[h], ident64(cx["pr"]), wait=gate)
            ev("vector", lambda e, c2=cx: e.tensor_scalar(
                c2["nbk"], c2["pb_t"].ap(0, 128, 0, HD), c2["nbh"], None, ALU.mult),
               bank=cx["pb_t"], wait=[(s_t, t_ktm)])
            v_nbk[h] = s_v.n
            ev("scalar", lambda e, c2=cx: e.copy(c2["ktm"], c2["pb_t"].ap(0, 128, 0, HD)),
               bank=cx["pb_t"], wait=[(s_t, t_ktm)])
            s_ktm = s_s.n
            t_kf0 = tr(cx["pb_t"], cx["ktm"], I128, wait=[(s_s, s_ktm)])
            ev("vector", lambda e, c2=cx: e.tensor_tensor(
                c2["dkf"], c2["pb_t"].ap(0, 64, 0, CH), c2["lbh"], ALU.mult),
               bank=cx["pb_t"], wait=[(s_t, t_kf0)])
            v_dkf[h] = s_v.n
            t_yt = tr(cx["pb_t"], cx["yp"](0), I128, wait=[(s_v, v_y[h])],
                      start=True, stop=True)
            ev("vector", lambda e, c2=cx: e.tensor_copy(c2["ytp"](0), c2["pb_t"].ap(0, 128, 0, CH)),
               bank=cx["pb_t"], wait=[(s_t, t_yt)])
            sqdep[(h, "yt0")] = [(s_v, s_v.n)]
            sqdep[(h, 0)] = [(s_v, v_y[h])]

        tr(pb_q, bpair, I128,
           wait=head_dep[h0] + head_dep[h1] + list(static_dep), c0=0,
           start=True, stop=False)
        mm(pb_q, ca["dkf"], sst0, False, False, wait=[(s_v, v_dkf[h0])], c0=0, cn=HD)
        t_rt = mm(pb_q, cb["dkf"], sst1, False, True, wait=[(s_v, v_dkf[h1])],
                  c0=64, cn=HD)
        ev("scalar", lambda e: e.copy(rtp, pb_q.ap(0, 128, 0, CH)), wait=[(s_t, t_rt)])
        s_rt = s_s.n

        apply_dep = [(s_s, s_rt)]
        cur = [rtp]

        def emit_apply(l):
            X = cur[0]
            w = list(apply_dep) + sqdep[(h0, l)] + sqdep[(h1, l)]
            mm(pb_q, ca["yp"](l), X[:, 0:HD], True, False, wait=w, c0=128, cn=HD)
            t_ap = mm(pb_q, cb["yp"](l), X[:, HD:2 * HD], False, True, c0=192, cn=HD)
            dst = xch(l)
            ev("vector", lambda e, ds=dst, x2=X: e.tensor_tensor(
                ds, pb_q.ap(0, 128, 128, CH), x2, ALU.add),
               wait=[(s_t, t_ap)])
            apply_dep[:] = [(s_v, s_v.n)]
            cur[0] = dst

        for lvl in range(1, 7):
            for cx in (ca, cb):
                h = cx["h"]
                t_sq = mm(cx["pb_a"], cx["ytp"](lvl - 1), cx["yp"](lvl - 1), True, True,
                          wait=sqdep[(h, lvl - 1)] + sqdep[(h, "yt0" if lvl == 1 else f"yt{lvl-1}")],
                          cn=CH)
                ev("vector", lambda e, c2=cx, l2=lvl: e.tensor_copy(
                    c2["yp"](l2), c2["pb_a"].ap(0, 128, 0, CH)),
                   bank=cx["pb_a"], wait=[(s_t, t_sq)])
                sqdep[(h, lvl)] = [(s_v, s_v.n)]
                if lvl < 6:
                    t_sqt = mm(cx["pb_t"], cx["yp"](lvl - 1), cx["ytp"](lvl - 1),
                               True, True,
                               wait=sqdep[(h, lvl - 1)] + sqdep[(h, "yt0" if lvl == 1 else f"yt{lvl-1}")],
                               cn=CH)
                    ev("vector", lambda e, c2=cx, l2=lvl: e.tensor_copy(
                        c2["ytp"](l2), c2["pb_t"].ap(0, 128, 0, CH)),
                       bank=cx["pb_t"], wait=[(s_t, t_sqt)])
                    sqdep[(h, f"yt{lvl}")] = [(s_v, s_v.n)]
            emit_apply(lvl - 1)
        emit_apply(6)
        wt0 = cur[0][:, 0:HD]
        wt1 = cur[0][:, HD:2 * HD]
        mm(pb_q, ca["nal"], wt0, True, False, wait=list(apply_dep), c0=256, cn=HD)
        t_o = mm(pb_q, cb["nal"], wt1, False, True, c0=320, cn=HD)
        ev("vector", lambda e: e.tensor_tensor(otp, pb_q.ap(0, 128, 256, CH), rtp,
                                               ALU.add),
           wait=[(s_t, t_o)])
        v_ot = s_v.n
        t_of = tr(pb_q, otp, I128, wait=[(s_v, v_ot)], c0=384, start=True, stop=True)
        ev("vector", lambda e: e.tensor_copy(bpair, pb_q.ap(0, 128, 384, CH)),
           wait=[(s_t, t_of)] + list(dbg_gate[0]))
        v_b = s_v.n
        mm(pb_s, ca["nbk"], wt0, True, False, wait=[(s_v, v_nbk[h0])],
           c0=0, cn=HD, pn=64)
        t_s2 = mm(pb_s, cb["nbk"], wt1, False, True, wait=[(s_v, v_nbk[h1])],
                  c0=64, cn=HD, pn=64)
        ev("vector", lambda e: e.scalar_tensor_tensor(
            sst0, sst0, PH(2, h0)[:64, :], pb_s.ap(0, 64, 0, HD),
            ALU.mult, ALU.add), wait=[(s_t, t_s2)])
        ev("vector", lambda e: e.scalar_tensor_tensor(
            sst1, sst1, PH(2, h1)[:64, :], pb_s.ap(0, 64, 64, HD),
            ALU.mult, ALU.add))
        head_dep[h0] = [(s_v, s_v.n)]
        head_dep[h1] = [(s_v, s_v.n)]
        slot_free_t[0] = s_t.n
        slot_free_t[1] = s_t.n
        pb_q.free = [(s_v, v_b), (s_s, s_s.n)]
        pb_s.free = [(s_v, s_v.n)]

    dbg_gate = [()]

    def dump_sw(off, sw, wait):
        t0 = sw * TSW
        for c in range(NCK):
            prog.op("gpsimd", lambda e, c2=c: e.dma_start(
                out=dbg_ap[c2 * 128:(c2 + 1) * 128, t0:t0 + TSW],
                in_=A(off, 0, 128, c2 * TSW, TSW)),
                wait=(wait if c == 0 else []), inc=[(s_dbg, 16)], drain=False)
        dbg_gate[0] = ((s_dbg, s_dbg.n),)

    MU = strow[:1, 0:TSW]
    RR = strow[:1, TSW:2 * TSW]

    for sw in range(SW):
        t0g = sw * TSW
        xw = [(s_t, x_cons["t"]), (s_s, x_cons["s"]), (s_v, x_cons["v"])]
        for c in range(NCK):
            dma("sync", xsw(c), xt[c * 128:(c + 1) * 128, t0g:t0g + TSW],
                wait=(xw if c == 0 else ()), sem=s_ldx)
        x_ld = s_ldx.n

        wf = [(s_ldx, x_ld), (s_t, b_cons["t"]), (s_s, b_cons["s"])]
        if sw == 0:
            wf += [(s_v, setup_v), (s_s, setup_s)]

        def ev_static(oc, bank, tl):
            ev("vector", lambda e, o2=oc, b2=bank: e.tensor_copy(bsw(o2), b2.ap()),
               bank=bank, wait=[(s_t, tl)])
        proj(wT, NCK, NCK, lambda i: xsw(i), [pbank[0], pbank[1]], ev_static,
             wait_first=wf)
        static_v = s_v.n
        x_cons["t"] = s_t.n
        if debug == "static":
            dump_sw(o_bsw, sw, [(s_v, static_v)])

        b6, b7 = pbank[6], pbank[7]
        for c in range(NCK):
            half = c % 2
            sq = sqt[:, half * TSW: half * TSW + TSW]
            w = [(s_ldx, x_ld), (s_t, sq_rd[half])]
            if c == 0 and sw == 0:
                w += [(s_v, setup_v), (s_s, setup_s)]
            prog.op("scalar", lambda e, s2=sq, c2=c: e.activation(
                s2, xsw(c2), AF.Square), wait=w, inc=[(s_s, 1)])
            ssq = s_s.n
            sq_rd[half] = mm(b6, c_sb["headmask"][:, c * H:(c + 1) * H], sq,
                             c == 0, c == NCK - 1, wait=[(s_s, ssq)], pn=16)
        t_ss = s_t.n
        prog.op("scalar", lambda e: e.activation(st16[:, 0:TSW], b6.ap(0, 16, 0, TSW),
                                                 AF.Sqrt, bias=c_eps[:16, 0:1], scale=1.0 / HD),
                wait=[(s_t, t_ss)], inc=[(s_s, 1)])
        b6.free = [(s_s, s_s.n)]
        s_r = s_s.n
        prog.op("vector", lambda e: e.reciprocal(st16[:, TSW:2 * TSW], st16[:, 0:TSW]),
                wait=[(s_s, s_r)], inc=[(s_v, 1)])
        v_r = s_v.n
        for c in range(NCK):
            t_bc = mm(b7, c_sb["headsel"][:, c * 128:(c + 1) * 128],
                      st16[:, TSW:2 * TSW], True, True, wait=[(s_v, v_r)])
            ev("vector", lambda e, c2=c: e.scalar_tensor_tensor(
                ksw(c2), xsw(c2), c_knw[:, :], b7.ap(), ALU.mult, ALU.mult),
               bank=b7, wait=[(s_t, t_bc)] + list(dbg_gate[0]))
        k_v = s_v.n
        x_cons["s"] = s_s.n
        x_cons["v"] = max(x_cons["v"], s_v.n)
        if debug == "k":
            dump_sw(o_ksw, sw, [(s_v, k_v)])

        for ciw in range(TSW // CH):
            for hp in range(0, H, 2):
                ca = delta_ctx(hp, 0)
                cb = delta_ctx(hp + 1, 1)
                delta_pair_emit(ca, cb, ciw, [(s_v, k_v)], [(s_v, static_v)])
                emit_split_job()
                emit_split_job()
        ddep = [(s_v, s_v.n), (s_s, s_s.n), (s_t, s_t.n)]
        if debug == "attnpre":
            dump_sw(o_bsw, sw, ddep)

        v_st = ln_stats(lambda c: bsw(c), MU, RR, ddep, pbank[0], pbank[1])
        mm(pbank[0], ONESR, MU, True, True, wait=[(s_v, v_st)])
        t_bc = mm(pbank[1], ONESR, RR, True, True)
        for c in range(NCK):
            w = ([(s_t, t_bc)] + list(dbg_gate[0])) if c == 0 else []
            prog.op("vector", lambda e, c2=c: e.tensor_sub(bsw(c2), bsw(c2),
                                                           pbank[0].ap()), wait=w)
            prog.op("vector", lambda e, c2=c: e.scalar_tensor_tensor(
                bsw(c2), bsw(c2), c_onw[:, c2:c2 + 1], pbank[1].ap(),
                ALU.mult, ALU.mult))
            prog.op("vector", lambda e, c2=c: e.scalar_tensor_tensor(
                bsw(c2), bsw(c2), c_onb[:, c2:c2 + 1], xsw(c2), ALU.add, ALU.add),
                inc=[(s_v, 1)])
        v_z = s_v.n
        x_cons["v"] = s_v.n
        pbank[0].free = [(s_v, v_z)]
        pbank[1].free = [(s_v, v_z)]
        v_st2 = ln_stats(lambda c: bsw(c), MU, RR, [(s_v, v_z)], pbank[0], pbank[1])
        mm(pbank[0], ONESR, MU, True, True, wait=[(s_v, v_st2)])
        t_bc2 = mm(pbank[1], ONESR, RR, True, True)
        for c in range(NCK):
            w = [(s_t, t_bc2)] if c == 0 else []
            xhc = x1h[:, c * L + t0g: c * L + t0g + TSW]
            xlc = x1l[:, c * L + t0g: c * L + t0g + TSW]
            x1sc = A(o_x1s, 0, 128, (c % 2) * TSW, TSW)
            prog.op("vector", lambda e, c2=c, xs=x1sc: e.tensor_sub(
                xs, bsw(c2), pbank[0].ap()), wait=w)
            prog.op("vector", lambda e, xs=x1sc: e.tensor_mul(xs, xs,
                                                              pbank[1].ap()), inc=[(s_v, 1)])
            vv = s_v.n
            prog.op("scalar", lambda e, x2=xhc, xs=x1sc: e.copy(x2, xs),
                    wait=[(s_v, vv)], inc=[(s_s, 1)])
            ss2 = s_s.n
            prog.op("vector", lambda e, x2=xlc, xh2=xhc, xs=x1sc: e.tensor_sub(
                x2, xs, xh2), wait=[(s_s, ss2)], inc=[(s_v, 1)])
        x1_done[sw] = s_v.n
        x1_done_s = s_s.n
        pbank[0].free = [(s_v, s_v.n)]
        pbank[1].free = [(s_v, s_v.n)]
        b_cons["t"] = s_t.n
        b_cons["s"] = s_s.n

    while split_jobs:
        emit_split_job()
    attn_all = ([(s_t, s_t.n), (s_v, s_v.n), (s_s, s_s.n)]
                + [(s_wspl[i], s_wspl[i].n) for i in range(NSLOT)])
    for _sl in range(NSLOT):
        ring_free[_sl] = s_t.n
        ring_wsp_rd[_sl] = s_wspl[_sl].n
    if debug == "x1":
        for c in range(NCK):
            prog.op("gpsimd", lambda e, c2=c: e.dma_start(
                out=dbg_ap[c2 * 128:(c2 + 1) * 128, :], in_=x1h[:, c2 * L:(c2 + 1) * L].bitcast(F32)[:, 0:L]) if False else e.dma_start(
                out=dbg_ap[c2 * 128:(c2 + 1) * 128, :], in_=x1h[:, c2 * L:(c2 + 1) * L]),
                wait=(attn_all if c == 0 else []), inc=[(s_dbg, 16)], drain=False)

    def comb_ap(p0, pn, c0, cn):
        return A(o_comb, p0, pn, c0, cn)

    prog.op("vector", lambda e: e.tensor_copy(gwh[:, :], gwsb[:, :]),
            wait=[(s_ldc, ld_const)])
    prog.op("vector", lambda e: e.tensor_sub(gwl[:, :], gwsb[:, :], gwh[:, :]),
            inc=[(s_v, 1)])
    gw_split_v = s_v.n
    for tt in range(SW):
        t0g = tt * TSW
        b0, b1 = pbank[0], pbank[1]
        t_lg = 0
        for i in range(NCK):
            w = (list(attn_all) + [(s_v, gw_split_v)]) if (tt == 0 and i == 0) else []
            xh2 = x1h[:, i * L + t0g: i * L + t0g + TSW]
            xl2 = x1l[:, i * L + t0g: i * L + t0g + TSW]
            mm(b0, gwh[:, i * E:(i + 1) * E], xh2, i == 0, False, wait=w, pn=E)
            mm(b0, gwh[:, i * E:(i + 1) * E], xl2, False, False, pn=E)
            mm(b0, gwl[:, i * E:(i + 1) * E], xh2, False, False, pn=E)
            t_lg = mm(b0, gwl[:, i * E:(i + 1) * E], xl2, False, i == NCK - 1, pn=E)
        sc = A(o_sgh, 0, 8, 0, TSW)
        ev("scalar", lambda e: e.activation(sc, b0.ap(0, 8, 0, TSW), AF.Sigmoid),
           bank=b0, wait=[(s_t, t_lg)])
        s_sc = s_s.n
        for q in range(TSW // 128):
            tb = A(o_t8, 0, 128, (tt * 4 + q) * E, E)
            t_tr = tr(b1, sc[:8, q * 128:(q + 1) * 128], c_sb["i128"][:8, :8],
                      wait=[(s_s, s_sc)])
            ev("vector", lambda e, t3=tb: e.tensor_copy(t3, b1.ap(0, 128, 0, E)),
               bank=b1, wait=[(s_t, t_tr)])
            m1 = A(o_r8, 0, 128, 0, 1)
            i1 = A(o_r8, 0, 128, 1, 1)
            m2 = A(o_r8, 0, 128, 2, 1)
            den = A(o_r8, 0, 128, 3, 1)
            mk1 = A(o_r8, 0, 128, 8, E)
            mk2 = A(o_r8, 0, 128, 16, E)
            s2t = A(o_r8, 0, 128, 24, E)
            V = "vector"
            prog.op(V, lambda e, t3=tb: e.tensor_reduce(m1, t3, AX.X, ALU.max))
            prog.op(V, lambda e, t3=tb: e.tensor_scalar(mk1, t3, m1, None, ALU.is_ge))
            prog.op(V, lambda e: e.tensor_mul(mk1, mk1, c_sb["revidx8"][:, :]))
            prog.op(V, lambda e: e.tensor_reduce(i1, mk1, AX.X, ALU.max))
            prog.op(V, lambda e: e.tensor_scalar(i1, i1, -8.0, -1.0, ALU.add, ALU.mult))
            prog.op(V, lambda e: e.tensor_scalar(mk1, c_sb["idxrow8"][:, :], i1,
                                                 None, ALU.is_equal))
            prog.op(V, lambda e, t3=tb: e.scalar_tensor_tensor(s2t, mk1, -1e30, t3,
                                                               ALU.mult, ALU.add))
            prog.op(V, lambda e: e.tensor_reduce(m2, s2t, AX.X, ALU.max))
            prog.op(V, lambda e: e.tensor_scalar(mk2, s2t, m2, None, ALU.is_ge))
            prog.op(V, lambda e: e.tensor_mul(mk2, mk2, c_sb["revidx8"][:, :]))
            prog.op(V, lambda e: e.tensor_reduce(i1, mk2, AX.X, ALU.max))
            prog.op(V, lambda e: e.tensor_scalar(i1, i1, -8.0, -1.0, ALU.add, ALU.mult))
            prog.op(V, lambda e: e.tensor_scalar(mk2, c_sb["idxrow8"][:, :], i1,
                                                 None, ALU.is_equal))
            prog.op(V, lambda e: e.tensor_add(den, m1, m2))
            prog.op(V, lambda e: e.tensor_scalar_add(den, den, 1e-20))
            prog.op(V, lambda e: e.reciprocal(den, den))
            prog.op(V, lambda e: e.tensor_scalar_mul(den, den, TOPK_SCALE))
            prog.op(V, lambda e: e.tensor_mul(m1, m1, den))
            prog.op(V, lambda e: e.tensor_mul(m2, m2, den))
            prog.op(V, lambda e: e.tensor_scalar(mk1, mk1, m1, None, ALU.mult))
            prog.op(V, lambda e, t3=tb: e.scalar_tensor_tensor(t3, mk2, m2, mk1,
                                                               ALU.mult, ALU.add),
                    inc=[(s_v, 1)])
            v_cb = s_v.n
            t_cb = tr(b1, tb, I128, wait=[(s_v, v_cb)], pn=8)
            ev("vector", lambda e, q2=q, t2=t0g: e.tensor_copy(
                comb_ap(0, 8, t2 + q2 * 128, 128), b1.ap(0, 8, 0, 128)),
               bank=b1, wait=[(s_t, t_cb)])
    comb_v = s_v.n

    sgh_rd_t = [0]
    acc_out_v = [0]

    for tt in range(SW):
        if SKIP_FFN:
            break
        t0g = tt * TSW

        def x1ap(i, t2=t0g):
            return (x1h[:, i * L + t2: i * L + t2 + TSW],
                    x1l[:, i * L + t2: i * L + t2 + TSW])

        def accap(c):
            return A(o_acc, 0, 128, c * TSW, TSW)

        def sgh(c):
            return A(o_sgh, 0, 128, c * TSW, TSW)

        def ev_g(oc, bank, tl):
            ev("scalar", lambda e, o2=oc, b2=bank: e.activation(
                sgh(o2), b2.ap(), AF.Silu), bank=bank,
               wait=[(s_t, tl), (s_t, sgh_rd_t[0])])
        proj3((ws_gTh, ws_gTl), 8, 8, x1ap, [pbank[0], pbank[1]], ev_g,
              wait_first=[(s_v, comb_v)])
        sg_s = s_s.n
        def hhap(oc):
            return (hhh[:, oc * TSW:(oc + 1) * TSW], hhl[:, oc * TSW:(oc + 1) * TSW])

        def ev_u(oc, bank, tl):
            ev("vector", lambda e, o2=oc, b2=bank: e.tensor_mul(
                sgh(o2), sgh(o2), b2.ap()), bank=bank,
               wait=[(s_t, tl), (s_s, sg_s)])
            vv = s_v.n
            prog.op("scalar", lambda e, o2=oc: e.copy(hhap(o2)[0], sgh(o2)),
                    wait=[(s_v, vv), (s_t, sgh_rd_t[0])], inc=[(s_s, 1)])
            ss2 = s_s.n
            prog.op("vector", lambda e, o2=oc: e.tensor_sub(
                hhap(o2)[1], sgh(o2), hhap(o2)[0]), wait=[(s_s, ss2)], inc=[(s_v, 1)])
        proj3((ws_uTh, ws_uTl), 8, 8, x1ap, [pbank[2], pbank[3]], ev_u)
        hh_v = s_v.n
        def ev_d(oc, bank, tl):
            xa = x1ap(oc)
            prog.op("vector", lambda e, o2=oc, x2=xa: e.tensor_add(
                accap(o2), x2[0], x2[1]),
                wait=[(s_t, tl), (s_v, acc_out_v[0])])
            ev("vector", lambda e, o2=oc, b2=bank: e.tensor_add(
                accap(o2), accap(o2), b2.ap()), bank=bank)
        proj3((ws_dTh, ws_dTl), 8, 8, hhap, [pbank[4], pbank[5]], ev_d,
              wait_first=[(s_v, hh_v)])
        sgh_rd_t[0] = s_t.n

        for e_i in range(E):
            b7 = pbank[7]
            t_cbb = mm(b7, c_sb["esel"][:, e_i * 128:(e_i + 1) * 128],
                       comb_ap(0, 8, t0g, TSW), True, True,
                       wait=[(s_v, comb_v)])

            def ev_ge(oc, bank, tl):
                ev("scalar", lambda e, o2=oc, b2=bank: e.activation(
                    sgh(o2), b2.ap(), AF.Silu), bank=bank,
                   wait=[(s_t, tl), (s_t, sgh_rd_t[0])])
            proj3((w_gTh[e_i * D:(e_i + 1) * D, :], w_gTl[e_i * D:(e_i + 1) * D, :]), 4, 8, x1ap,
                  [pbank[0], pbank[1]], ev_ge)
            sg_s2 = s_s.n

            def ev_ue(oc, bank, tl):
                ev("vector", lambda e, o2=oc, b2=bank: e.tensor_mul(
                    sgh(o2), sgh(o2), b2.ap()), bank=bank,
                   wait=[(s_t, tl), (s_s, sg_s2), (s_t, t_cbb)])
                prog.op("vector", lambda e, o2=oc: e.tensor_mul(
                    sgh(o2), sgh(o2), b7.ap()), inc=[(s_v, 1)])
                vv = s_v.n
                prog.op("scalar", lambda e, o2=oc: e.copy(hhap(o2)[0], sgh(o2)),
                        wait=[(s_v, vv), (s_t, sgh_rd_t[0])], inc=[(s_s, 1)])
                ss2 = s_s.n
                prog.op("vector", lambda e, o2=oc: e.tensor_sub(
                    hhap(o2)[1], sgh(o2), hhap(o2)[0]), wait=[(s_s, ss2)],
                    inc=[(s_v, 1)])
            proj3((w_uTh[e_i * D:(e_i + 1) * D, :], w_uTl[e_i * D:(e_i + 1) * D, :]), 4, 8, x1ap,
                  [pbank[2], pbank[3]], ev_ue)
            hh_v2 = s_v.n
            b7.free = [(s_v, hh_v2)]

            def ev_de(oc, bank, tl):
                ev("vector", lambda e, o2=oc, b2=bank: e.tensor_add(
                    accap(o2), accap(o2), b2.ap()), bank=bank, wait=[(s_t, tl)])
            proj3((w_dTh[e_i * IE:(e_i + 1) * IE, :], w_dTl[e_i * IE:(e_i + 1) * IE, :]), 8, 4, hhap,
                  [pbank[4], pbank[5]], ev_de, wait_first=[(s_v, hh_v2)])
            sgh_rd_t[0] = s_t.n
        routed_v = s_v.n

        v_st3 = ln_stats(lambda c: accap(c), MU, RR, [(s_v, routed_v)],
                         pbank[0], pbank[1])
        mm(pbank[0], ONESR, MU, True, True, wait=[(s_v, v_st3)])
        t_bc4 = mm(pbank[1], ONESR, RR, True, True)
        ost = A(o_outst, 0, 128, 0, TSW)
        for c in range(NCK):
            w = [(s_t, t_bc4)] if c == 0 else []
            w = w + [(s_out, 16 * (tt * NCK + c))]
            prog.op("vector", lambda e, c2=c, o2=ost: e.tensor_sub(
                o2, accap(c2), pbank[0].ap()), wait=w)
            prog.op("vector", lambda e, o2=ost: e.tensor_mul(o2, o2, pbank[1].ap()),
                    inc=[(s_v, 1)])
            v_o = s_v.n
            prog.op("gpsimd", lambda e, c2=c, o2=ost, t2=t0g: e.dma_start(
                out=yt[c2 * 128:(c2 + 1) * 128, t2:t2 + TSW], in_=o2),
                wait=[(s_v, v_o)], inc=[(s_out, 16)], drain=False)
        pbank[0].free = [(s_v, s_v.n)]
        pbank[1].free = [(s_v, s_v.n)]
        acc_out_v[0] = s_v.n

    if debug == "comb":
        prog.op("gpsimd", lambda e: e.dma_start(out=dbg_ap, in_=comb_ap(0, 8, 0, L)),
                wait=[(s_v, comb_v)], inc=[(s_dbg, 16)], drain=False)

    if SKIP_FFN:
        for c in range(NCK):
            prog.op("gpsimd", lambda e, c2=c: e.dma_start(
                out=yt[c2 * 128:(c2 + 1) * 128, :], in_=x1f[:, c2 * L:(c2 + 1) * L]),
                wait=(attn_all if c == 0 else []), inc=[(s_out, 16)], drain=False)
    prog.op("gpsimd", lambda e: e.nop(),
            wait=[(s_out, s_out.n)] + ([(s_dbg, s_dbg.n)] if s_dbg.n else []))

    with stack:
        with contextlib.ExitStack() as ss2:
            for s2 in sems:
                s2.handle = ss2.enter_context(nc.semaphore(s2.name))
            with nc.Block() as block:
                @block.tensor
                def _(eng):
                    prog.emit("tensor", eng)

                @block.vector
                def _(eng):
                    prog.emit("vector", eng)

                @block.scalar
                def _(eng):
                    prog.emit("scalar", eng)

                @block.gpsimd
                def _(eng):
                    prog.emit("gpsimd", eng)

                @block.sync
                def _(eng):
                    prog.emit("sync", eng)
    return nc


def make_inmaps(L0_attn_x, L0_attn_W, L0_attn_k_norm_w, L0_attn_o_norm_w,
                L0_attn_o_norm_b, L0_attn_eta_raw, L0_attn_lam_logit,
                L0_ffn_gate_weight, L0_ffn_w_gate, L0_ffn_w_up, L0_ffn_w_down,
                L0_ffn_ws_gate, L0_ffn_ws_up, L0_ffn_ws_down):
    f32 = np.float32
    ac = np.ascontiguousarray
    cc = host_constants()
    shared = {
        "wT": ac(np.asarray(L0_attn_W, f32).T),
        "ws_gT": ac(np.asarray(L0_ffn_ws_gate, f32).T),
        "ws_uT": ac(np.asarray(L0_ffn_ws_up, f32).T),
        "ws_dT": ac(np.asarray(L0_ffn_ws_down, f32).T),
        "w_gT": ac(np.transpose(np.asarray(L0_ffn_w_gate, f32), (0, 2, 1))).reshape(E * D, IE),
        "w_uT": ac(np.transpose(np.asarray(L0_ffn_w_up, f32), (0, 2, 1))).reshape(E * D, IE),
        "w_dT": ac(np.transpose(np.asarray(L0_ffn_w_down, f32), (0, 2, 1))).reshape(E * IE, D),
        "gwT": ac(np.asarray(L0_ffn_gate_weight, f32).T),
        "onwT": ac(np.asarray(L0_attn_o_norm_w, f32).reshape(NCK, 128).T),
        "onbT": ac(np.asarray(L0_attn_o_norm_b, f32).reshape(NCK, 128).T),
        "knw": ac(np.tile(np.asarray(L0_attn_k_norm_w, f32), 2)[:, None]),
        "eta_row": ac(np.asarray(L0_attn_eta_raw, f32)[None, :]),
        "laml_row": ac(np.asarray(L0_attn_lam_logit, f32)[None, :]),
    }
    for n, a in cc.items():
        shared[n] = ac(a, dtype=f32)
    in_maps = []
    for b in range(8):
        m = dict(shared)
        m["xt"] = ac(np.asarray(L0_attn_x[b], f32).T)
        in_maps.append(m)
    return in_maps


def kernel(**inputs):
    nc = build_program()
    in_maps = make_inmaps(**inputs)
    res = run_bass_kernel_spmd(nc, in_maps, core_ids=list(range(8)))
    out = np.empty((8, L, D), dtype=np.float32)
    for b in range(8):
        out[b] = res.results[b]["yt"].T
    return out
